# version 13
# baseline (speedup 1.0000x reference)
import numpy as np
import ml_dtypes
from contextlib import ExitStack
import concourse.bass as bass
import concourse.mybir as mybir
from concourse.bass_utils import run_bass_kernel_spmd

F32 = mybir.dt.float32
BF16 = mybir.dt.bfloat16
AF = mybir.ActivationFunctionType
ALU = mybir.AluOpType
AX = mybir.AxisListType

B = 16
BL = 2
S = 2048
DM = 1024
TA = B * S
TL = BL * S
NE = 2048
NEXP = 64
PSLOT = 32
H0, HD0 = 16, 64
H1, HD1 = 8, 128
DH = 4096
LAMBDA_INIT = 0.8
SCALE0 = 1.0 / np.sqrt(64.0)
SCALE1 = 128.0 ** (-0.5)
NCORES = 8

ENGINES = ("pe", "act", "dve", "sync", "gp")


NSLOT = 8


class Prog:

    CAST_SEMS = ("cphi", "cew", "cffn")

    def __init__(self, nc, stack):
        self.nc = nc
        self.stack = stack
        self.semids = ["pe", "act", "dve"] + \
            [f"sync{i}" for i in range(NSLOT)] + [f"gp{i}" for i in range(NSLOT)] + \
            [f"actd{i}" for i in range(NSLOT)] + list(self.CAST_SEMS)
        self.sems = {s: stack.enter_context(nc.semaphore(f"sem_{s}")) for s in self.semids}
        self.ops = {e: [] for e in ENGINES}
        self.semcnt = {s: 0 for s in self.semids}
        self.dma_i = {"sync": 0, "gp": 0, "act": 0}
        self.clock = {e: {s: 0 for s in self.semids} for e in ENGINES}
        self.snaps = {s: [] for s in self.semids}
        self.writer = {}
        self.readers = {}
        self.nwaits = 0
        self.bg = []
        self.bg_i = 0

    def drain_bg(self, k):
        while k > 0 and self.bg_i < len(self.bg):
            self.bg[self.bg_i]()
            self.bg_i += 1
            k -= 1

    def _merge(self, eng, semid, val):
        snaps = self.snaps[semid]
        best = None
        lo, hi = 0, len(snaps)
        while lo < hi:
            mid = (lo + hi) // 2
            if snaps[mid][0] <= val:
                lo = mid + 1
            else:
                hi = mid
        if lo > 0:
            best = snaps[lo - 1][1]
        ck = self.clock[eng]
        if best is not None:
            for k, v in best.items():
                if v > ck.get(k, 0):
                    ck[k] = v
        if val > ck.get(semid, 0):
            ck[semid] = val

    def _wait(self, eng, semid, val):
        if self.clock[eng].get(semid, 0) >= val:
            return
        self.ops[eng].append(("w", semid, val))
        self.nwaits += 1
        self._merge(eng, semid, val)

    def step(self, eng, fn, reads=(), writes=(), dma=False, tag=None, dsem=None):
        need = {}
        for k in reads:
            w = self.writer.get(k)
            if w is not None:
                need[w[0]] = max(need.get(w[0], 0), w[1])
        for k in writes:
            w = self.writer.get(k)
            if w is not None:
                need[w[0]] = max(need.get(w[0], 0), w[1])
            for s2, c2 in self.readers.get(k, {}).items():
                need[s2] = max(need.get(s2, 0), c2)
        for s2, v2 in need.items():
            self._wait(eng, s2, v2)
        if dsem is not None:
            assert dma
            semid = dsem
            inc = 16
        elif dma:
            assert eng in ("sync", "gp", "act")
            slot = self.dma_i[eng] % NSLOT
            self.dma_i[eng] += 1
            semid = f"{eng}{slot}" if eng != "act" else f"actd{slot}"
            if self.semcnt[semid] > 0:
                self._wait(eng, semid, self.semcnt[semid])
            inc = 16
        else:
            semid = eng
            inc = 1
        self.semcnt[semid] += inc
        mycount = self.semcnt[semid]
        self.ops[eng].append(("op", fn, semid, inc))
        if eng == "pe" and not dma:
            self.clock["pe"]["pe"] = mycount
        snap = dict(self.clock[eng])
        snap[semid] = mycount
        self.snaps[semid].append((mycount, snap))
        for k in writes:
            self.writer[k] = (semid, mycount)
            self.readers[k] = {}
        for k in reads:
            self.readers.setdefault(k, {})
            cur = self.readers[k].get(semid, 0)
            self.readers[k][semid] = max(cur, mycount)
        return semid, mycount

    def barrier(self, full=False):
        for e in ENGINES:
            for s in self.semids:
                if s in self.CAST_SEMS and not full:
                    continue
                if self.semcnt[s] > 0:
                    self._wait(e, s, self.semcnt[s])

    def finalize(self, block):
        sems = self.sems

        def make_runner(eng):
            items = self.ops[eng]

            def run(engobj):
                for it in items:
                    if it[0] == "w":
                        engobj.wait_ge(sems[it[1]], it[2])
                    else:
                        last = it[1](engobj)
                        last.then_inc(sems[it[2]], it[3])
            return run

        block.tensor(make_runner("pe"))
        block.scalar(make_runner("act"))
        block.vector(make_runner("dve"))
        block.sync(make_runner("sync"))
        block.gpsimd(make_runner("gp"))


def ceil_div(a, b):
    return (a + b - 1) // b


def mm_group(P, psum_ap, pairs, psum_key, read_keys, start=True, stop=True,
             nmax=512):
    ncols = psum_ap.shape[-1]
    nsl = ceil_div(ncols, nmax)

    def fn(t):
        last = None
        for i, (l, r) in enumerate(pairs):
            for j in range(nsl):
                c0, c1 = j * nmax, min((j + 1) * nmax, ncols)
                last = t.matmul(psum_ap[:, c0:c1], l, r[:, c0:c1],
                                start=(start and i == 0), stop=(stop and i == len(pairs) - 1))
        return last
    P.step("pe", fn, reads=tuple(read_keys), writes=(psum_key,))



def sbuf(P, name, shape, dtype):
    return P.stack.enter_context(P.nc.sbuf_tensor(name, shape, dtype))


def psum(P, name, shape):
    return P.stack.enter_context(P.nc.psum_tensor(name, shape, F32))


DT_NP = {F32: np.float32, BF16: ml_dtypes.bfloat16}


def ph_qkv0(P, io, stack):
    w_ap = io["L0_attn_w_qkv"]
    xn_ap = io["xnloc_tm"]
    out_ap = io["qkv0_fm"]
    NF = 3 * DM
    w_sb = sbuf(P, "q0_w", [128, 8 * NF], BF16)
    for dc in range(8):
        P.step("gp", (lambda dc=dc: (lambda g: g.dma_start(
            out=w_sb[:, dc * NF:(dc + 1) * NF], in_=w_ap[dc * 128:(dc + 1) * 128, :])))(),
            writes=("q0:w",), dma=True)
    fm = sbuf(P, "q0_fm", [128, 8 * TL], BF16)
    for dc in range(8):
        P.step("sync", (lambda dc=dc: (lambda e: e.dma_start(
            out=fm[:, dc * TL:(dc + 1) * TL],
            in_=xn_ap[0:TL, dc * 128:(dc + 1) * 128], transpose=True)))(),
            reads=("xnloc_tm:all",), writes=("q0:fm",), dma=True)
    ps = [psum(P, "q0_psA", [128, 1024]), psum(P, "q0_psB", [128, 1024])]
    ev = sbuf(P, "q0_ev", [128, 2 * 1024], BF16)
    i = 0
    for fc in range(NF // 128):
        for tq in range(TL // 1024):
            pi = i % 2
            i += 1
            pairs = [(w_sb[:, dc * NF + fc * 128:dc * NF + (fc + 1) * 128],
                      fm[:, dc * TL + tq * 1024:dc * TL + (tq + 1) * 1024])
                     for dc in range(8)]
            mm_group(P, ps[pi][:], pairs, f"ps:q0{pi}", ("q0:w", "q0:fm"))
            P.step("act", (lambda pi=pi: (lambda a: a.activation(
                ev[:, pi * 1024:(pi + 1) * 1024], ps[pi][:], AF.Copy)))(),
                reads=(f"ps:q0{pi}",), writes=(f"q0:ev{pi}",))
            P.step("gp", (lambda fc=fc, tq=tq, pi=pi: (lambda g: g.dma_start(
                out=out_ap[fc * 128:(fc + 1) * 128, tq * 1024:(tq + 1) * 1024],
                in_=ev[:, pi * 1024:(pi + 1) * 1024])))(),
                reads=(f"q0:ev{pi}",), writes=("qkv0_fm:all",), dma=True)


def ph_attn0(P, io, stack):
    qkv_ap = io["qkv0_fm"]
    wo_ap = io["L0_attn_wo"]
    out_ap = io["attn0"]
    NKT = S // 128
    QH = S // 2
    idt = sbuf(P, "a0_idt", [128, 128], BF16)
    idtf = sbuf(P, "a0_idtf", [128, 128], F32)
    P.step("sync", (lambda: (lambda e: e.dma_start(out=idt[:], in_=io["IDT"][:])))(),
           writes=("a0:idt",), dma=True)
    P.step("sync", (lambda: (lambda e: e.dma_start(out=idtf[:], in_=io["IDTF"][:])))(),
           writes=("a0:idtf",), dma=True)
    ones_f = sbuf(P, "a0_onesf", [1, 128], F32)
    ones_c = sbuf(P, "a0_onesc", [128, 1], BF16)
    P.step("dve", (lambda: (lambda v: v.memset(ones_f[:], 1.0)))(), writes=("a0:onesf",))
    P.step("dve", (lambda: (lambda v: v.memset(ones_c[:], 1.0)))(), writes=("a0:onesc",))
    lamv = sbuf(P, "a0_lamv", [1, 4 * 64], F32)
    for i, nm in enumerate(["L0_attn_lambda_q1", "L0_attn_lambda_q2",
                            "L0_attn_lambda_k1", "L0_attn_lambda_k2"]):
        P.step("sync", (lambda i=i, nm=nm: (lambda e: e.dma_start(
            out=lamv[:, i * 64:(i + 1) * 64], in_=io[nm][None, :])))(),
            writes=("a0:lamv",), dma=True)
    prod = sbuf(P, "a0_prod", [1, 128], F32)
    dots = sbuf(P, "a0_dots", [1, 2], F32)
    expd = sbuf(P, "a0_expd", [1, 2], F32)
    nlam1 = sbuf(P, "a0_nlam1", [1, 1], F32)
    P.step("dve", (lambda: (lambda v: v.tensor_mul(
        prod[:].rearrange("o (a f) -> o a f", a=2),
        lamv[:, 0:128].rearrange("o (a f) -> o a f", a=2),
        lamv[:, 128:256].rearrange("o (a f) -> o a f", a=2))))(),
        reads=("a0:lamv",), writes=("a0:prod",))
    P.step("dve", (lambda: (lambda v: v.reduce_sum(
        dots[:], prod[:].rearrange("o (a f) -> o a f", a=2), axis=AX.X)))(),
        reads=("a0:prod",), writes=("a0:dots",))
    P.step("act", (lambda: (lambda a: a.activation(expd[:], dots[:], AF.Exp)))(),
           reads=("a0:dots",), writes=("a0:expd",))
    P.step("dve", (lambda: (lambda v: v.tensor_sub(nlam1[:], expd[:, 1:2], expd[:, 0:1])))(),
           reads=("a0:expd",), writes=("a0:nlam",))
    P.step("dve", (lambda: (lambda v: v.tensor_scalar(nlam1[:], nlam1[:], -LAMBDA_INIT, None, ALU.add)))(),
           reads=("a0:nlam",), writes=("a0:nlam",))
    nlam128 = sbuf(P, "a0_nlamb", [128, 1], F32)
    psQK = psum(P, "a0_psQK", [128, 2048])
    psV = psum(P, "a0_psV", [128, 1024])
    psT = psum(P, "a0_psT", [128, 1024])
    P.step("pe", (lambda: (lambda t: t.matmul(psT[0:128, 0:1], ones_f[:, 0:128], nlam1[:],
                                              start=True, stop=True)))(),
           reads=("a0:nlam", "a0:onesf"), writes=("ps:a0T",))
    P.step("act", (lambda: (lambda a: a.activation(nlam128[:], psT[0:128, 0:1], AF.Copy)))(),
           reads=("ps:a0T",), writes=("a0:nlamb",))
    nwrow = sbuf(P, "a0_nwrow", [1, H0 * HD0], F32)
    nwb = sbuf(P, "a0_nwb", [128, H0 * HD0], F32)
    P.step("sync", (lambda: (lambda e: e.dma_start(
        out=nwrow[:], in_=io["L0_attn_norm_w"][:].rearrange("a b -> (a b)")[None, :])))(),
        writes=("a0:nw",), dma=True)
    P.step("pe", (lambda: (lambda t: (
        t.matmul(psT[:, 0:512], ones_f[:, 0:128], nwrow[:, 0:512], start=True, stop=True),
        t.matmul(psT[:, 512:1024], ones_f[:, 0:128], nwrow[:, 512:1024], start=True, stop=True))[-1]))(),
        reads=("a0:nw", "a0:onesf"), writes=("ps:a0T",))
    P.step("act", (lambda: (lambda a: a.activation(nwb[:], psT[:, 0:1024], AF.Copy,
                                                   scale=1.0 - LAMBDA_INIT)))(),
           reads=("ps:a0T",), writes=("a0:nwb",))
    wo_sb = sbuf(P, "a0_wo", [128, 8 * DM], BF16)
    for c in range(8):
        P.step("gp", (lambda c=c: (lambda g: g.dma_start(
            out=wo_sb[:, c * DM:(c + 1) * DM], in_=wo_ap[c * 128:(c + 1) * 128, :])))(),
            writes=("a0:wo",), dma=True)

    q_sb = sbuf(P, "a0_q", [128, S], BF16)
    k_sb = sbuf(P, "a0_k", [128, S], BF16)
    vt_b = sbuf(P, "a0_vt", [128, NKT * DM], BF16)
    vb65 = sbuf(P, "a0_vb65", [128, NKT * 65], BF16)
    Eb = sbuf(P, "a0_E", [128, 2 * 4 * 512], BF16)
    o1 = sbuf(P, "a0_o1", [65, 512], F32)
    o2 = sbuf(P, "a0_o2", [65, 512], F32)
    o1T = sbuf(P, "a0_o1T", [128, 4 * 65], F32)
    o2T = sbuf(P, "a0_o2T", [128, 4 * 65], F32)
    rs1T = sbuf(P, "a0_rs1T", [128, 4], F32)
    rs2T = sbuf(P, "a0_rs2T", [128, 4], F32)
    t1T = sbuf(P, "a0_t1T", [128, 4 * 64], F32)
    osbT = sbuf(P, "a0_osbT", [128, 4 * 64], F32)
    sqT = sbuf(P, "a0_sqT", [128, 4 * 64], F32)
    msT = sbuf(P, "a0_msT", [128, 4], F32)
    rstdT = sbuf(P, "a0_rstdT", [128, 4], F32)
    onT = sbuf(P, "a0_onT", [128, 4 * 64], F32)
    oft = sbuf(P, "a0_oft", [128, 8 * S], BF16)
    evt = sbuf(P, "a0_evt", [128, DM], F32)
    P.step("dve", (lambda: (lambda v: v.memset(
        vb65[:].rearrange("p (kt f) -> p kt f", f=65)[:, :, 64:65], 1.0)))(),
        writes=("a0:vb65c",))

    for b in range(BL):
        for kt in range(NKT):
            P.step("sync", (lambda b=b, kt=kt: (lambda e: e.dma_start(
                out=vt_b[:, kt * DM:(kt + 1) * DM],
                in_=qkv_ap[2 * DM:3 * DM, b * S + kt * 128:b * S + (kt + 1) * 128],
                transpose=True)))(),
                reads=("qkv0_fm:all",), writes=("a0:vt",), dma=True)
        for h in range(H0):
            for half in range(2):
                P.step("sync", (lambda b=b, h=h, half=half: (lambda e: e.dma_start(
                    out=q_sb[64 * half:64 * (half + 1), :],
                    in_=qkv_ap[h * 64:(h + 1) * 64, b * S:(b + 1) * S])))(),
                    reads=("qkv0_fm:all",), writes=("a0:q",), dma=True)
                P.step("sync", (lambda b=b, h=h, half=half: (lambda e: e.dma_start(
                    out=k_sb[64 * half:64 * (half + 1), :],
                    in_=qkv_ap[DM + h * 64:DM + (h + 1) * 64, b * S:(b + 1) * S])))(),
                    reads=("qkv0_fm:all",), writes=("a0:k",), dma=True)
            P.step("dve", (lambda h=h: (lambda v: v.tensor_copy(
                vb65[:].rearrange("p (kt f) -> p kt f", f=65)[:, :, 0:64],
                vt_b[:].rearrange("p (kt f) -> p kt f", f=DM)[:, :, h * 64:(h + 1) * 64])))(),
                reads=("a0:vt", "a0:vb65c"), writes=("a0:vb65",))
            for qc in range(4):
                q0 = qc * 512
                for ktp in range(8):
                    ktA, ktB = 2 * ktp, 2 * ktp + 1
                    ep = ktp % 2
                    Ev = Eb[:, ep * 2048:(ep + 1) * 2048]
                    kE = f"a0:E{ep}"

                    def fnqk(t, ktA=ktA, ktB=ktB, q0=q0):
                        rows = ((0, ktA), (32, ktA), (64, ktB), (96, ktB))
                        last = None
                        for i, (r, kt) in enumerate(rows):
                            last = t.matmul(psQK[:, i * 512:(i + 1) * 512],
                                            k_sb[r:r + 32, kt * 128:(kt + 1) * 128],
                                            q_sb[r:r + 32, q0:q0 + 512],
                                            start=True, stop=True,
                                            tile_position=(r, 0))
                        return last
                    P.step("pe", fnqk, reads=("a0:k", "a0:q"), writes=("ps:a0QK",))
                    P.step("act", (lambda Ev=Ev: (lambda a: [
                        a.activation(Ev[:, i * 512:(i + 1) * 512],
                                     psQK[:, i * 512:(i + 1) * 512], AF.Exp,
                                     scale=SCALE0) for i in range(4)][-1]))(),
                        reads=("ps:a0QK",), writes=(kE,))

                    def fnpv(t, ktA=ktA, ktB=ktB, ktp=ktp, Ev=Ev):
                        st, sp = (ktp == 0), (ktp == 7)
                        t.matmul(psV[0:65, 0:512], vb65[:, ktA * 65:(ktA + 1) * 65],
                                 Ev[:, 0:512], start=st, stop=False)
                        t.matmul(psV[0:65, 0:512], vb65[:, ktB * 65:(ktB + 1) * 65],
                                 Ev[:, 1024:1536], start=False, stop=sp)
                        t.matmul(psV[0:65, 512:1024], vb65[:, ktA * 65:(ktA + 1) * 65],
                                 Ev[:, 512:1024], start=st, stop=False)
                        return t.matmul(psV[0:65, 512:1024], vb65[:, ktB * 65:(ktB + 1) * 65],
                                        Ev[:, 1536:2048], start=False, stop=sp)
                    P.step("pe", fnpv, reads=("a0:vb65", kE), writes=("ps:a0V",))
                P.step("act", (lambda: (lambda a: a.activation(o1[:], psV[0:65, 0:512], AF.Copy)))(),
                       reads=("ps:a0V",), writes=("a0:o1",))
                P.step("act", (lambda: (lambda a: a.activation(o2[:], psV[0:65, 512:1024], AF.Copy)))(),
                       reads=("ps:a0V",), writes=("a0:o2",))
                for src, key, off, dstT, dk in ((o1, "a0:o1", 0, o1T, "a0:o1T"),
                                                (o2, "a0:o2", 512, o2T, "a0:o2T")):
                    def fntr(t, src=src, off=off):
                        last = None
                        for cc in range(4):
                            last = t.matmul(psT[:, off + cc * 65:off + (cc + 1) * 65],
                                            src[0:65, cc * 128:(cc + 1) * 128],
                                            idtf[0:65, 0:65], start=True, stop=True)
                        return last
                    P.step("pe", fntr, reads=(key, "a0:idtf"), writes=("ps:a0T",))
                    P.step("act", (lambda off=off, dstT=dstT: (lambda a: a.activation(
                        dstT[:], psT[:, off:off + 260], AF.Copy)))(),
                        reads=("ps:a0T",), writes=(dk,))
                o1T_v = o1T[:].rearrange("p (c f) -> p c f", f=65)
                o2T_v = o2T[:].rearrange("p (c f) -> p c f", f=65)
                P.step("dve", (lambda: (lambda v: v.reciprocal(rs1T[:], o1T_v[:, :, 64:65].squeeze(2))))(),
                       reads=("a0:o1T",), writes=("a0:rs1",))
                P.step("dve", (lambda: (lambda v: v.reciprocal(rs2T[:], o2T_v[:, :, 64:65].squeeze(2))))(),
                       reads=("a0:o2T",), writes=("a0:rs2",))
                t1T_v = t1T[:].rearrange("p (c f) -> p c f", f=64)
                osbT_v = osbT[:].rearrange("p (c f) -> p c f", f=64)
                P.step("dve", (lambda: (lambda v: v.tensor_tensor(
                    t1T_v, o1T_v[:, :, 0:64],
                    rs1T[:].unsqueeze(2).broadcast_to([128, 4, 64]), ALU.mult)))(),
                    reads=("a0:o1T", "a0:rs1"), writes=("a0:t1",))
                P.step("dve", (lambda: (lambda v: v.tensor_tensor(
                    osbT_v, o2T_v[:, :, 0:64],
                    rs2T[:].unsqueeze(2).broadcast_to([128, 4, 64]), ALU.mult)))(),
                    reads=("a0:o2T", "a0:rs2"), writes=("a0:osb",))
                P.step("dve", (lambda: (lambda v: v.scalar_tensor_tensor(
                    osbT[:], osbT[:], nlam128[:], t1T[:], ALU.mult, ALU.add)))(),
                    reads=("a0:osb", "a0:t1", "a0:nlamb"), writes=("a0:osb",))
                P.step("dve", (lambda: (lambda v: v.tensor_mul(sqT[:], osbT[:], osbT[:])))(),
                       reads=("a0:osb",), writes=("a0:sq",))
                P.step("dve", (lambda: (lambda v: v.reduce_sum(
                    msT[:], sqT[:].rearrange("p (c f) -> p c f", f=64), axis=AX.X)))(),
                    reads=("a0:sq",), writes=("a0:ms",))
                P.step("dve", (lambda: (lambda v: v.tensor_scalar(
                    msT[:], msT[:], 1.0 / 64.0, 1e-6, ALU.mult, ALU.add)))(),
                    reads=("a0:ms",), writes=("a0:ms",))
                P.step("act", (lambda: (lambda a: a.activation(msT[:], msT[:], AF.Ln)))(),
                       reads=("a0:ms",), writes=("a0:ms",))
                P.step("act", (lambda: (lambda a: a.activation(rstdT[:], msT[:], AF.Exp,
                                                               scale=-0.5)))(),
                       reads=("a0:ms",), writes=("a0:rstd",))
                onT_v = onT[:].rearrange("p (c f) -> p c f", f=64)
                P.step("dve", (lambda h=h: (lambda v: v.tensor_tensor(
                    onT_v, osbT_v,
                    nwb[:, h * 64:(h + 1) * 64].unsqueeze(1).broadcast_to([128, 4, 64]),
                    ALU.mult)))(),
                    reads=("a0:osb", "a0:nwb"), writes=("a0:on",))
                P.step("dve", (lambda: (lambda v: v.tensor_tensor(
                    onT_v, onT_v,
                    rstdT[:].unsqueeze(2).broadcast_to([128, 4, 64]), ALU.mult)))(),
                    reads=("a0:on", "a0:rstd"), writes=("a0:on",))
                for cc in range(4):
                    cq = qc * 4 + cc
                    c = cq % 8
                    r0 = 2 * h + cq // 8
                    P.step("dve", (lambda cc=cc, c=c, r0=r0: (lambda v: v.tensor_copy(
                        oft[:, c * S:(c + 1) * S].rearrange("p (d r) -> p d r", r=32)[:, :, r0:r0 + 1],
                        onT[:, cc * 64:(cc + 1) * 64].unsqueeze(2))))(),
                        reads=("a0:on",), writes=(f"a0:oft",))
        for st in range(S // 128):
            pi = st % 2
            pp = psQK[:, pi * 1024:(pi + 1) * 1024]
            pairs = [(oft[:, c * S + st * 128:c * S + (st + 1) * 128],
                      wo_sb[:, c * DM:(c + 1) * DM]) for c in range(8)]
            mm_group(P, pp, pairs,
                     "ps:a0QK", ("a0:oft", "a0:wo"))
            P.step("act", (lambda pi=pi, pp=pp: (lambda a: a.activation(
                evt[:], pp, AF.Copy)))(),
                reads=("ps:a0QK",), writes=("a0:evt",))
            P.step("gp", (lambda b=b, st=st: (lambda g: g.dma_start(
                out=out_ap[b * S + st * 128:b * S + (st + 1) * 128, :], in_=evt[:])))(),
                reads=("a0:evt",), writes=("attn0:all",), dma=True)


def run_standalone(emit_fn, ins_spec, outs_spec, inputs, n_cores=1, sim=False,
                   interm_spec=()):
    nc = bass.Bass("TRN2", target_bir_lowering=False)
    io = {}
    for name, shape, dt in ins_spec:
        io[name] = nc.dram_tensor(name, list(shape), dt, kind="ExternalInput").ap()
    for name, shape, dt in outs_spec:
        io[name] = nc.dram_tensor(name, list(shape), dt, kind="ExternalOutput").ap()
    for name, shape, dt in interm_spec:
        io[name] = nc.dram_tensor(name, list(shape), dt, kind="Internal").ap()
    if "xn_tm" in io:
        io["xnloc_tm"] = io["xn_tm"][0:TL, :]
    with ExitStack() as stack:
        P = Prog(nc, stack)
        emit_fn(P, io, stack)
        P.barrier()
        block = stack.enter_context(nc.Block())
        P.finalize(block)
    if sim:
        from concourse.bass_interp import CoreSim
        s = CoreSim(nc)
        for name, _, _ in ins_spec:
            s.tensor(name)[:] = inputs[name]
        s.simulate(check_with_hw=False)
        return {name: np.array(s.tensor(name)) for name, _, _ in outs_spec}
    res = run_bass_kernel_spmd(nc, [inputs] * n_cores, core_ids=list(range(n_cores)))
    return res.results[0]


def ph_experts(P, io, stack):
    xt_ap = io["Xt_bf"]
    ew_ap = io["ew_bf"]
    eb_ap = io["L0_ffn_expert_b"]
    yt_ap = io["Yt_bf"]
    XtT = sbuf(P, "ex_XtT", [128, 8 * NE], BF16)
    for dc in range(8):
        P.step("sync", (lambda dc=dc: (lambda e: e.dma_start(
            out=XtT[:, dc * NE:(dc + 1) * NE],
            in_=xt_ap[0:NE, dc * 128:(dc + 1) * 128], transpose=True)))(),
            reads=("Xt_bf:all",), writes=("ex:XtT",), dma=True)
    eb_row = sbuf(P, "ex_eb", [1, 16 * DM], BF16)
    ones32 = sbuf(P, "ex_ones", [1, 32], BF16)
    P.step("dve", (lambda: (lambda v: v.memset(ones32[:], 1.0)))(), writes=("ex:ones",))
    wbuf = [sbuf(P, f"ex_w{i}", [128, 8 * DM], BF16) for i in range(2)]
    pE = [psum(P, "ex_psA", [128, 1024]), psum(P, "ex_psB", [128, 1024])]
    yt_sb = sbuf(P, "ex_yt", [32, 2 * DM], BF16)
    for n in range(NEXP):
        wi = n % 2
        if n % 16 == 0:
            P.step("gp", (lambda n=n: (lambda g: g.dma_start(
                out=eb_row[:],
                in_=eb_ap[n:n + 16, :].rearrange("a b -> (a b)")[None, :])))(),
                writes=("ex:eb",), dma=True)
        P.step("sync", (lambda n=n, w=wbuf[wi]: (lambda g: g.dma_start(
            out=w[:].rearrange("p (c f) -> p c f", c=8),
            in_=ew_ap[n].rearrange("(c p) f -> p c f", p=128))))(),
            reads=("ew_bf:all",), writes=(f"ex:w{wi}",), dma=True)

        def fn(t, n=n, wi=wi):
            nn = n % 16
            last = t.matmul(pE[wi][0:32, 0:512], ones32[:],
                            eb_row[:, nn * DM:nn * DM + 512], start=True, stop=False)
            last = t.matmul(pE[wi][0:32, 512:1024], ones32[:],
                            eb_row[:, nn * DM + 512:(nn + 1) * DM], start=True, stop=False)
            for dc in range(8):
                for j in range(2):
                    last = t.matmul(
                        pE[wi][0:32, j * 512:(j + 1) * 512],
                        XtT[:, dc * NE + n * 32:dc * NE + (n + 1) * 32],
                        wbuf[wi][:, dc * DM + j * 512:dc * DM + (j + 1) * 512],
                        start=False, stop=(dc == 7))
            return last
        P.step("pe", fn, reads=(f"ex:w{wi}", "ex:XtT", "ex:eb", "ex:ones"),
               writes=(f"ps:ex{wi}",))
        P.step("act", (lambda wi=wi: (lambda a: a.activation(
            yt_sb[:, wi * DM:(wi + 1) * DM], pE[wi][0:32, :], AF.Copy)))(),
            reads=(f"ps:ex{wi}",), writes=(f"ex:yt{wi}",))
        P.step("gp", (lambda n=n, wi=wi: (lambda g: g.dma_start(
            out=yt_ap[:].rearrange("(q n) d -> q n d", n=64)[:, n, :],
            in_=yt_sb[:, wi * DM:(wi + 1) * DM])))(),
            reads=(f"ex:yt{wi}",), writes=("Yt_bf:all",), dma=True)


def ph_ctrans(P, io, stack):
    ctm_ap = io["C_tm"]
    cp_ap = io["Cp"]
    idt = sbuf(P, "ct_idt", [128, 128], BF16)
    P.step("sync", (lambda: (lambda e: e.dma_start(out=idt[:], in_=io["IDT"][:])))(),
           writes=("ct:idt",), dma=True)
    cin = sbuf(P, "ct_in", [128, NE], BF16)
    grp = sbuf(P, "ct_grp", [128, 16 * 1024], BF16)
    psA = psum(P, "ct_psA", [128, 1024])
    psB = psum(P, "ct_psB", [128, 1024])
    ntile = TL // 128
    for g in range(ntile // 8):
        for ti in range(8):
            t = g * 8 + ti
            P.step("sync", (lambda t=t: (lambda e: e.dma_start(
                out=cin[:], in_=ctm_ap[t * 128:(t + 1) * 128, :])))(),
                reads=("C_tm:all",), writes=("ct:in",), dma=True)
            for half, pp in ((0, psA), (1, psB)):
                def fn(tr, half=half, pp=pp):
                    last = None
                    for j in range(8):
                        ec = half * 8 + j
                        last = tr.matmul(pp[:, j * 128:(j + 1) * 128],
                                         cin[:, ec * 128:(ec + 1) * 128], idt[:],
                                         start=True, stop=True)
                    return last
                P.step("pe", fn, reads=("ct:in", "ct:idt"), writes=(f"ps:ct{half}",))
                P.step("dve", (lambda ti=ti, half=half, pp=pp: (lambda v: v.tensor_copy(
                    grp[:].rearrange("p (ec a f) -> p ec a f", ec=16, a=8)[:, half * 8:(half + 1) * 8, ti, :],
                    pp[:].rearrange("p (j f) -> p j f", j=8))))(),
                    reads=(f"ps:ct{half}",), writes=("ct:grp",))
        for ec in range(16):
            P.step("gp", (lambda g=g, ec=ec: (lambda gp: gp.dma_start(
                out=cp_ap[ec, :, g * 1024:(g + 1) * 1024],
                in_=grp[:].rearrange("p (ec f) -> p ec f", ec=16)[:, ec, :])))(),
                reads=("ct:grp",), writes=("Cp:all",), dma=True)


def ph_y(P, io, stack):
    cp_ap = io["Cp"]
    yt_ap = io["Yt_bf"]
    x_ap = io["x_loc"]
    a0_ap = io["attn0"]
    x1_ap = io["x1"]
    xn1_ap = io["xn1_tm"]
    ys = sbuf(P, "y_ys", [128, 16 * DM], BF16)
    for ec in range(16):
        P.step("sync", (lambda ec=ec: (lambda e: e.dma_start(
            out=ys[:, ec * DM:(ec + 1) * DM], in_=yt_ap[ec * 128:(ec + 1) * 128, :])))(),
            reads=("Yt_bf:all",), writes=("y:ys",), dma=True)
    cps = sbuf(P, "y_cp", [128, 2 * 512], BF16)
    psY = [psum(P, f"y_ps{i}", [128, DM]) for i in range(4)]
    xb = sbuf(P, "y_xb", [128, 4 * DM], F32)
    ab = sbuf(P, "y_ab", [128, 4 * DM], F32)
    x1b = sbuf(P, "y_x1", [128, 4 * DM], F32)
    xn1b = sbuf(P, "y_xn1", [128, 4 * DM], BF16)
    lnb = make_ln_bufs(P, "y_ln", 4)
    for g in range(TL // 512):
        P.step("sync", (lambda g=g: (lambda e: e.dma_start(
            out=xb[:].rearrange("p (a f) -> p a f", a=4),
            in_=x_ap[g * 512:(g + 1) * 512, :].rearrange("(a p) f -> p a f", p=128))))(),
            writes=tuple(f"y:x{i}" for i in range(4)), dma=True)
        P.step("sync", (lambda g=g: (lambda e: e.dma_start(
            out=ab[:].rearrange("p (a f) -> p a f", a=4),
            in_=a0_ap[g * 512:(g + 1) * 512, :].rearrange("(a p) f -> p a f", p=128))))(),
            reads=("attn0:all",), writes=tuple(f"y:a{i}" for i in range(4)), dma=True)
        for ec in range(16):
            ci = ec % 2
            P.step("sync", (lambda g=g, ec=ec, ci=ci: (lambda e: e.dma_start(
                out=cps[:, ci * 512:(ci + 1) * 512],
                in_=cp_ap[ec, :, g * 512:(g + 1) * 512])))(),
                reads=("Cp:all",), writes=(f"y:cp{ci}",), dma=True)
            for ti in range(4):
                mm_group(P, psY[ti][:],
                         [(cps[:, ci * 512 + ti * 128:ci * 512 + (ti + 1) * 128],
                           ys[:, ec * DM:(ec + 1) * DM])],
                         f"ps:y{ti}", (f"y:cp{ci}", "y:ys"),
                         start=(ec == 0), stop=(ec == 15))
        for ti in range(4):
            P.step("dve", (lambda ti=ti: (lambda v: v.tensor_add(
                x1b[:, ti * DM:(ti + 1) * DM], psY[ti][:], xb[:, ti * DM:(ti + 1) * DM])))(),
                reads=(f"ps:y{ti}", f"y:x{ti}"), writes=(f"y:x1{ti}",))
            P.step("dve", (lambda ti=ti: (lambda v: v.tensor_add(
                x1b[:, ti * DM:(ti + 1) * DM], x1b[:, ti * DM:(ti + 1) * DM],
                ab[:, ti * DM:(ti + 1) * DM])))(),
                reads=(f"y:x1{ti}", f"y:a{ti}"), writes=(f"y:x1{ti}",))
        P.step("gp", (lambda g=g: (lambda gp: gp.dma_start(
            out=x1_ap[g * 512:(g + 1) * 512, :].rearrange("(a p) f -> p a f", p=128),
            in_=x1b[:].rearrange("p (a f) -> p a f", a=4))))(),
            reads=tuple(f"y:x1{i}" for i in range(4)), writes=("x1:all",), dma=True)
        ln_tiles(P, lnb, [x1b[:, i * DM:(i + 1) * DM] for i in range(4)],
                 [f"y:x1{i}" for i in range(4)],
                 [xn1b[:, i * DM:(i + 1) * DM] for i in range(4)],
                 [f"y:xn1{i}" for i in range(4)])
        P.step("gp", (lambda g=g: (lambda gp: gp.dma_start(
            out=xn1_ap[g * 512:(g + 1) * 512, :].rearrange("(a p) f -> p a f", p=128),
            in_=xn1b[:].rearrange("p (a f) -> p a f", a=4))))(),
            reads=tuple(f"y:xn1{i}" for i in range(4)), writes=("xn1_tm:all",), dma=True)


def ph_qkv1(P, io, stack):
    w_ap = io["L1_attn_w_qkv"]
    xn_ap = io["xn1_tm"]
    out_ap = io["qkv6_fm"]
    NF = 6144
    fm = sbuf(P, "q1_fm", [128, 8 * TL], BF16)
    for dc in range(8):
        P.step("sync", (lambda dc=dc: (lambda e: e.dma_start(
            out=fm[:, dc * TL:(dc + 1) * TL],
            in_=xn_ap[0:TL, dc * 128:(dc + 1) * 128], transpose=True)))(),
            reads=("xn1_tm:all",), writes=("q1:fm",), dma=True)
    GW = 1024
    wg = [sbuf(P, f"q1_w{i}", [128, 8 * GW], BF16) for i in range(2)]
    ps = [psum(P, "q1_psA", [128, 1024]), psum(P, "q1_psB", [128, 1024])]
    ev = sbuf(P, "q1_ev", [128, 2 * 1024], BF16)
    i = 0
    for fg in range(NF // GW):
        wi = fg % 2
        for dc in range(8):
            P.step("gp", (lambda fg=fg, dc=dc, w=wg[wi]: (lambda g: g.dma_start(
                out=w[:, dc * GW:(dc + 1) * GW],
                in_=w_ap[dc * 128:(dc + 1) * 128, fg * GW:(fg + 1) * GW])))(),
                writes=(f"q1:w{wi}",), dma=True)
        for fc in range(GW // 128):
            for tq in range(TL // 1024):
                pi = i % 2
                i += 1
                pairs = [(wg[wi][:, dc * GW + fc * 128:dc * GW + (fc + 1) * 128],
                          fm[:, dc * TL + tq * 1024:dc * TL + (tq + 1) * 1024])
                         for dc in range(8)]
                mm_group(P, ps[pi][:], pairs, f"ps:q1{pi}", (f"q1:w{wi}", "q1:fm"))
                P.step("act", (lambda pi=pi: (lambda a: a.activation(
                    ev[:, pi * 1024:(pi + 1) * 1024], ps[pi][:], AF.Copy)))(),
                    reads=(f"ps:q1{pi}",), writes=(f"q1:ev{pi}",))
                P.step("gp", (lambda fg=fg, fc=fc, tq=tq, pi=pi: (lambda g: g.dma_start(
                    out=out_ap[fg * GW + fc * 128:fg * GW + (fc + 1) * 128,
                               tq * 1024:(tq + 1) * 1024],
                    in_=ev[:, pi * 1024:(pi + 1) * 1024])))(),
                    reads=(f"q1:ev{pi}",), writes=("qkv6_fm:all",), dma=True)


def ph_attn1(P, io, stack):
    qk_ap = io["qkv6_fm"]
    wout_ap = io["L1_attn_w_out"]
    out_ap = io["attn1"]
    NKT = S // 128
    QH = S // 2
    OQU, OKU, OVU, OQC, OKC, OVC = 0, 1024, 2048, 3072, 4096, 5120
    msl = sbuf(P, "a1_msl", [128, 128], F32)
    mle = sbuf(P, "a1_mle", [128, 128], BF16)
    P.step("sync", (lambda: (lambda e: e.dma_start(out=msl[:], in_=io["MSL"][:])))(),
           writes=("a1:msl",), dma=True)
    P.step("sync", (lambda: (lambda e: e.dma_start(out=mle[:], in_=io["MLE"][:])))(),
           writes=("a1:mle",), dma=True)
    ones_f = sbuf(P, "a1_onesf", [1, 128], F32)
    ones_c = sbuf(P, "a1_onesc", [128, 1], BF16)
    P.step("dve", (lambda: (lambda v: v.memset(ones_f[:], 1.0)))(), writes=("a1:onesf",))
    P.step("dve", (lambda: (lambda v: v.memset(ones_c[:], 1.0)))(), writes=("a1:onesc",))
    wout_sb = sbuf(P, "a1_wout", [128, 8 * DM], BF16)
    for c in range(8):
        P.step("gp", (lambda c=c: (lambda g: g.dma_start(
            out=wout_sb[:, c * DM:(c + 1) * DM], in_=wout_ap[c * 128:(c + 1) * 128, :])))(),
            writes=("a1:wout",), dma=True)
    vu_b = sbuf(P, "a1_vu", [128, NKT * DM], BF16)
    vc_b = sbuf(P, "a1_vc", [128, NKT * DM], BF16)
    qc_s = sbuf(P, "a1_qc", [128, S], BF16)
    kc_s = sbuf(P, "a1_kc", [128, S], BF16)
    qu_s = sbuf(P, "a1_qu", [128, S], BF16)
    ku_s = sbuf(P, "a1_ku", [128, S], BF16)
    sg = sbuf(P, "a1_sg", [128, 128], F32)
    la = sbuf(P, "a1_la", [128, 128], BF16)
    sg2 = sbuf(P, "a1_sg2", [128, 128], F32)
    corr = sbuf(P, "a1_corr", [128, NKT * 128], F32)
    Es = sbuf(P, "a1_E", [128, QH], BF16)
    rs_s = sbuf(P, "a1_rs", [1, QH], F32)
    rb_s = sbuf(P, "a1_rb", [128, QH], F32)
    o_all = sbuf(P, "a1_oall", [128, 8 * S], BF16)
    evt = sbuf(P, "a1_evt", [128, DM], F32)
    psA = psum(P, "a1_psA", [128, 1024])
    psO = psum(P, "a1_psO", [128, 1024])
    psS = psum(P, "a1_psS", [128, 2 * 512])
    for b in range(BL):
        for kt in range(NKT):
            for nm, off, buf, key in (("vu", OVU, vu_b, "a1:vu"), ("vc", OVC, vc_b, "a1:vc")):
                P.step("sync", (lambda off=off, b=b, kt=kt, buf=buf: (lambda e: e.dma_start(
                    out=buf[:, kt * DM:(kt + 1) * DM],
                    in_=qk_ap[off:off + DM, b * S + kt * 128:b * S + (kt + 1) * 128],
                    transpose=True)))(),
                    reads=("qkv6_fm:all",), writes=(key,), dma=True)
        for h in range(H1):
            for nm, off, buf, key in (("qc", OQC, qc_s, "a1:qc"), ("kc", OKC, kc_s, "a1:kc"),
                                      ("qu", OQU, qu_s, "a1:qu"), ("ku", OKU, ku_s, "a1:ku")):
                P.step("sync", (lambda off=off, b=b, h=h, buf=buf: (lambda e: e.dma_start(
                    out=buf[:], in_=qk_ap[off + h * 128:off + (h + 1) * 128,
                                          b * S:(b + 1) * S])))(),
                    reads=("qkv6_fm:all",), writes=(key,), dma=True)
            P.step("dve", (lambda: (lambda v: v.tensor_scalar(qc_s[:], qc_s[:], SCALE1, None, ALU.mult)))(),
                   reads=("a1:qc",), writes=("a1:qc",))
            P.step("dve", (lambda: (lambda v: v.tensor_scalar(qu_s[:], qu_s[:], SCALE1, None, ALU.mult)))(),
                   reads=("a1:qu",), writes=("a1:qu",))
            for qt in range(NKT):
                P.step("pe", (lambda qt=qt: (lambda t: t.matmul(
                    psA[:, 0:128], ku_s[:, qt * 128:(qt + 1) * 128],
                    qu_s[:, qt * 128:(qt + 1) * 128], start=True, stop=True)))(),
                    reads=("a1:ku", "a1:qu"), writes=("ps:a1A",))
                P.step("act", (lambda: (lambda a: a.activation(sg[:], psA[:, 0:128], AF.Sigmoid)))(),
                       reads=("ps:a1A",), writes=("a1:sg",))
                P.step("dve", (lambda: (lambda v: v.tensor_mul(la[:], sg[:], msl[:])))(),
                       reads=("a1:sg", "a1:msl"), writes=("a1:la",))
                P.step("pe", (lambda qt=qt, h=h: (lambda t: t.matmul(
                    psO[:, 0:128], vu_b[:, qt * DM + h * 128:qt * DM + (h + 1) * 128],
                    la[:], start=True, stop=True)))(),
                    reads=("a1:vu", "a1:la"), writes=("ps:a1O",))
                P.step("act", (lambda: (lambda a: a.activation(sg2[:], psO[:, 0:128], AF.Sigmoid)))(),
                       reads=("ps:a1O",), writes=("a1:sg2",))
                P.step("dve", (lambda qt=qt: (lambda v: v.tensor_mul(
                    corr[:, qt * 128:(qt + 1) * 128], psO[:, 0:128], sg2[:])))(),
                    reads=("ps:a1O", "a1:sg2"), writes=("a1:corr",))
            for qh in range(2):
                nkt_h = (qh + 1) * (QH // 128)
                cs_max = max(0, (nkt_h - 1) * 128 - qh * QH)
                if cs_max > 0:
                    P.step("dve", (lambda cs_max=cs_max: (lambda v: v.memset(
                        Es[:, 0:cs_max], 0.0)))(), writes=("a1:E",))
                for kt in range(nkt_h - 1, -1, -1):
                    cs = max(0, kt * 128 - qh * QH)
                    qsl = slice(qh * QH + cs, (qh + 1) * QH)
                    diag = (qh * QH <= kt * 128 < (qh + 1) * QH)
                    P.step("pe", (lambda kt=kt, cs=cs, qh=qh: (lambda t:
                        _smm(t, psA, kc_s, qc_s, kt, cs, qh, QH)))(),
                        reads=("a1:kc", "a1:qc"), writes=("ps:a1A",))
                    if diag:
                        dc0 = kt * 128 - qh * QH
                        P.step("dve", (lambda kt=kt, dc0=dc0: (lambda v: v.tensor_sub(
                            psA[:, dc0:dc0 + 128], psA[:, dc0:dc0 + 128],
                            corr[:, kt * 128:(kt + 1) * 128])))(),
                            reads=("ps:a1A", "a1:corr"), writes=("ps:a1A",))
                    P.step("act", (lambda cs=cs: (lambda a: a.activation(
                        Es[:, cs:QH], psA[:, cs:QH], AF.Exp)))(),
                        reads=("ps:a1A",), writes=("a1:E",))
                    if diag:
                        dc0 = kt * 128 - qh * QH
                        P.step("dve", (lambda dc0=dc0: (lambda v: v.tensor_mul(
                            Es[:, dc0:dc0 + 128], Es[:, dc0:dc0 + 128], mle[:])))(),
                            reads=("a1:E", "a1:mle"), writes=("a1:E",))
                    P.step("pe", (lambda kt=kt, h=h, st=(kt == nkt_h - 1), sp=(kt == 0): (lambda t:
                        _pvmm(t, psO, vc_b, Es, kt, h, QH, st, sp)))(),
                        reads=("a1:vc", "a1:E"), writes=("ps:a1O",))
                    P.step("pe", (lambda st=(kt == nkt_h - 1), sp=(kt == 0): (lambda t:
                        _summ(t, psS, ones_c, Es, QH, st, sp)))(),
                        reads=("a1:onesc", "a1:E"), writes=("ps:a1S",))
                P.step("dve", (lambda: (lambda v: v.reciprocal(rs_s[:, 0:QH], psS[0:1, 0:QH])))(),
                       reads=("ps:a1S",), writes=("a1:rs",))
                P.step("pe", (lambda: (lambda t: [
                    t.matmul(psS[0:128, c0:c0 + 512] if False else psS[:, c0:c0 + 512],
                             ones_f[:], rs_s[:, c0:c0 + 512], start=True, stop=True)
                    for c0 in range(0, QH, 512)][-1]))(),
                    reads=("a1:rs", "a1:onesf"), writes=("ps:a1S",))
                P.step("act", (lambda: (lambda a: a.activation(rb_s[:, 0:QH], psS[:, 0:QH], AF.Copy)))(),
                       reads=("ps:a1S",), writes=("a1:rb",))
                P.step("dve", (lambda h=h, qh=qh: (lambda v: v.tensor_mul(
                    o_all[:, h * S + qh * QH:h * S + (qh + 1) * QH], psO[:, 0:QH], rb_s[:, 0:QH])))(),
                    reads=("ps:a1O", "a1:rb"), writes=("a1:oall",))
        for st in range(S // 128):
            pairs = [(o_all[:, hh * S + st * 128:hh * S + (st + 1) * 128],
                      wout_sb[:, hh * DM:(hh + 1) * DM]) for hh in range(H1)]
            mm_group(P, psA[:, 0:DM], pairs, "ps:a1A", ("a1:oall", "a1:wout"))
            P.step("act", (lambda: (lambda a: a.activation(evt[:], psA[:, 0:DM], AF.Copy)))(),
                   reads=("ps:a1A",), writes=("a1:evt",))
            P.step("gp", (lambda b=b, st=st: (lambda g: g.dma_start(
                out=out_ap[b * S + st * 128:b * S + (st + 1) * 128, :], in_=evt[:])))(),
                reads=("a1:evt",), writes=("attn1:all",), dma=True)


def col_slices(cs, hi):
    out = []
    c0 = cs
    if c0 % 512:
        c1 = min((c0 // 512 + 1) * 512, hi)
        out.append((c0, c1))
        c0 = c1
    while c0 < hi:
        c1 = min(c0 + 512, hi)
        out.append((c0, c1))
        c0 = c1
    return out


def _smm(t, psA, kc_s, qc_s, kt, cs, qh, QH):
    last = None
    for c0, c1 in col_slices(cs, QH):
        last = t.matmul(psA[:, c0:c1], kc_s[:, kt * 128:(kt + 1) * 128],
                        qc_s[:, qh * QH + c0:qh * QH + c1], start=True, stop=True)
    return last


def _pvmm(t, psO, vc_b, Es, kt, h, QH, start, stop):
    last = None
    for c0 in range(0, QH, 512):
        last = t.matmul(psO[:, c0:c0 + 512], vc_b[:, kt * DM + h * 128:kt * DM + (h + 1) * 128],
                        Es[:, c0:c0 + 512], start=start, stop=stop)
    return last


def _summ(t, psS, ones_c, Es, QH, start, stop):
    last = None
    for c0 in range(0, QH, 512):
        last = t.matmul(psS[0:1, c0:c0 + 512], ones_c[:], Es[:, c0:c0 + 512], start=start, stop=stop)
    return last


def ph_ffn1(P, io, stack):
    xn_ap = io["xn1_tm"]
    x1_ap = io["x1"]
    a1_ap = io["attn1"]
    out_ap = io["OUT"]
    w1p_ap, w2p_ap = io["w1p"], io["w2p"]
    h1p_ap, h2p_ap = io["h1p"], io["h2p"]
    w3p_ap = io["w3p"]
    idtf = sbuf(P, "f_idtf", [128, 128], F32)
    P.step("sync", (lambda: (lambda e: e.dma_start(out=idtf[:], in_=io["IDTF"][:])))(),
           writes=("f:idtf",), dma=True)
    ones_f = sbuf(P, "f_onesf", [1, 128], F32)
    P.step("dve", (lambda: (lambda v: v.memset(ones_f[:], 1.0)))(), writes=("f:onesf",))
    s_raw = sbuf(P, "f_sraw", [32, 4 * 128], F32)
    ssb = sbuf(P, "f_s", [128, 4 * 32], F32)
    psS = psum(P, "f_psS", [128, 512])
    for i, nm in enumerate(["L1_ffn_w1_s", "L1_ffn_w2_s", "L1_ffn_h1_s", "L1_ffn_h2_s"]):
        P.step("sync", (lambda nm=nm, i=i: (lambda e: e.dma_start(
            out=s_raw[:, i * 128:(i + 1) * 128],
            in_=io[nm][:].rearrange("(c p) -> c p", p=128))))(),
            writes=("f:sraw",), dma=True)
    for i in range(4):
        P.step("pe", (lambda i=i: (lambda t: t.matmul(
            psS[:, i * 32:(i + 1) * 32], s_raw[:, i * 128:(i + 1) * 128],
            idtf[0:32, 0:32], start=True, stop=True)))(),
            reads=("f:sraw", "f:idtf"), writes=("ps:fS",))
    P.step("act", (lambda: (lambda a: a.activation(ssb[:], psS[:, 0:128], AF.Copy)))(),
           reads=("ps:fS",), writes=("f:s",))
    s3row = sbuf(P, "f_s3row", [1, DM], F32)
    w3row = sbuf(P, "f_w3row", [128, DM], F32)
    P.step("sync", (lambda: (lambda e: e.dma_start(out=s3row[:], in_=io["L1_ffn_w3_s"][None, :])))(),
           writes=("f:s3r",), dma=True)
    psW = psum(P, "f_psW", [128, 512])
    for j in range(2):
        P.step("pe", (lambda j=j: (lambda t: t.matmul(
            psW[:], ones_f[:], s3row[:, j * 512:(j + 1) * 512], start=True, stop=True)))(),
            reads=("f:s3r", "f:onesf"), writes=("ps:fW",))
        P.step("act", (lambda j=j: (lambda a: a.activation(
            w3row[:, j * 512:(j + 1) * 512], psW[:], AF.Copy)))(),
            reads=("ps:fW",), writes=("f:w3row",))
    T = 1024
    xfm = sbuf(P, "f_xfm", [128, 8 * T], BF16)
    ub = sbuf(P, "f_u", [128, 32 * T], BF16)
    gb = sbuf(P, "f_g", [128, 32 * T], BF16)
    pan = sbuf(P, "f_pan", [128, 32 * 512], BF16)
    ps2 = [psum(P, "f_psA", [128, T]), psum(P, "f_psB", [128, T])]
    evb = sbuf(P, "f_ev", [128, T], BF16)
    xtb = sbuf(P, "f_xt", [128, DM], F32)
    atb = sbuf(P, "f_at", [128, DM], F32)
    otb = sbuf(P, "f_ot", [128, DM], F32)
    npass = TL // T
    for p_i in range(npass):
        tok0 = p_i * T
        for dc in range(8):
            P.step("sync", (lambda dc=dc, tok0=tok0: (lambda e: e.dma_start(
                out=xfm[:, dc * T:(dc + 1) * T],
                in_=xn_ap[tok0:tok0 + T, dc * 128:(dc + 1) * 128], transpose=True)))(),
                reads=("xn1_tm:all",), writes=("f:xfm",), dma=True)
        for stage in range(2):
            wp_ap, wpn = ((w1p_ap, "w1p"), (w2p_ap, "w2p"))[stage]
            for og in range(8):
                for hf in range(2):
                    P.step("sync", (lambda og=og, hf=hf, wp_ap=wp_ap: (lambda e: e.dma_start(
                        out=pan[:, hf * 4 * 512:(hf + 1) * 4 * 512]
                            .rearrange("p (c o) -> p c o", c=4),
                        in_=wp_ap[hf * 4:(hf + 1) * 4, :, og * 512:(og + 1) * 512]
                            .rearrange("c p o -> p c o"))))(),
                        reads=(wpn + ":all",), writes=(f"f:pan{hf}",), dma=True)
                for oc4 in range(4):
                    oc = og * 4 + oc4
                    pi = oc % 2
                    for hf in range(2):
                        pairs = [(pan[:, ic * 512 + oc4 * 128:ic * 512 + (oc4 + 1) * 128],
                                  xfm[:, ic * T:(ic + 1) * T]) for ic in range(hf * 4, hf * 4 + 4)]
                        mm_group(P, ps2[pi][:], pairs, f"ps:f{pi}",
                                 (f"f:pan{hf}", "f:xfm"),
                                 start=(hf == 0), stop=(hf == 1))
                    if stage == 0:
                        P.step("act", (lambda oc=oc, pi=pi: (lambda a: a.activation(
                            ub[:, oc * T:(oc + 1) * T], ps2[pi][:], AF.Silu,
                            scale=ssb[:, oc:oc + 1])))(),
                            reads=(f"ps:f{pi}", "f:s"), writes=(f"f:ub{oc}",))
                    else:
                        P.step("act", (lambda oc=oc, pi=pi: (lambda a: a.activation(
                            evb[:], ps2[pi][:], AF.Copy,
                            scale=ssb[:, 32 + oc:32 + oc + 1])))(),
                            reads=(f"ps:f{pi}", "f:s"), writes=("f:ev",))
                        P.step("dve", (lambda oc=oc: (lambda v: v.tensor_mul(
                            ub[:, oc * T:(oc + 1) * T], ub[:, oc * T:(oc + 1) * T],
                            evb[:])))(),
                            reads=(f"f:ub{oc}", "f:ev"), writes=(f"f:ub{oc}",))
        for stage in range(2):
            hp_ap, hpn = ((h1p_ap, "h1p"), (h2p_ap, "h2p"))[stage]
            sc0 = 64 + stage * 32
            src, dst = ((ub, gb), (gb, ub))[stage]
            ksrc, kdst = (("f:ub", "f:gb"), ("f:gb", "f:ub"))[stage]
            for og in range(8):
                for hf in range(2):
                    P.step("sync", (lambda og=og, hf=hf, hp_ap=hp_ap: (lambda e: e.dma_start(
                        out=pan[:, hf * 16 * 512:(hf + 1) * 16 * 512]
                            .rearrange("p (c o) -> p c o", c=16),
                        in_=hp_ap[hf * 16:(hf + 1) * 16, :, og * 512:(og + 1) * 512]
                            .rearrange("c p o -> p c o"))))(),
                        reads=(hpn + ":all",), writes=(f"f:pan{hf}",), dma=True)
                for oc4 in range(4):
                    oc = og * 4 + oc4
                    pi = oc % 2
                    for hf in range(2):
                        pairs = [(pan[:, kc * 512 + oc4 * 128:kc * 512 + (oc4 + 1) * 128],
                                  src[:, kc * T:(kc + 1) * T]) for kc in range(hf * 16, hf * 16 + 16)]
                        mm_group(P, ps2[pi][:], pairs, f"ps:f{pi}",
                                 (f"f:pan{hf}",) + tuple(f"{ksrc}{kc}" for kc in range(hf * 16, hf * 16 + 16)),
                                 start=(hf == 0), stop=(hf == 1))
                    P.step("act", (lambda oc=oc, pi=pi, sc0=sc0, dst=dst: (lambda a: a.activation(
                        dst[:, oc * T:(oc + 1) * T], ps2[pi][:], AF.Gelu,
                        scale=ssb[:, sc0 + oc:sc0 + oc + 1])))(),
                        reads=(f"ps:f{pi}", "f:s"), writes=(f"{kdst}{oc}",))
        for kc in range(32):
            P.step("sync", (lambda kc=kc: (lambda e: e.dma_start(
                out=gb[:, kc * T:kc * T + DM],
                in_=w3p_ap[kc * 128:(kc + 1) * 128, :])))(),
                reads=("w3p:all",), writes=(f"f:gb{kc}",), dma=True)
        for ti in range(T // 128):
            pi = ti % 2
            pairs = [(ub[:, kc * T + ti * 128:kc * T + (ti + 1) * 128],
                      gb[:, kc * T:kc * T + DM]) for kc in range(32)]
            mm_group(P, ps2[pi][:, 0:DM], pairs, f"ps:f{pi}",
                     tuple(f"f:ub{kc}" for kc in range(32)) +
                     tuple(f"f:gb{kc}" for kc in range(32)))
            P.step("sync", (lambda ti=ti, tok0=tok0: (lambda e: e.dma_start(
                out=xtb[:], in_=x1_ap[tok0 + ti * 128:tok0 + (ti + 1) * 128, :])))(),
                reads=("x1:all",), writes=("f:xt",), dma=True)
            P.step("sync", (lambda ti=ti, tok0=tok0: (lambda e: e.dma_start(
                out=atb[:], in_=a1_ap[tok0 + ti * 128:tok0 + (ti + 1) * 128, :])))(),
                reads=("attn1:all",), writes=("f:at",), dma=True)
            P.step("dve", (lambda pi=pi: (lambda v: v.tensor_mul(
                otb[:], ps2[pi][:, 0:DM], w3row[:])))(),
                reads=(f"ps:f{pi}", "f:w3row"), writes=("f:ot",))
            P.step("dve", (lambda: (lambda v: v.tensor_add(otb[:], otb[:], xtb[:])))(),
                   reads=("f:ot", "f:xt"), writes=("f:ot",))
            P.step("dve", (lambda: (lambda v: v.tensor_add(otb[:], otb[:], atb[:])))(),
                   reads=("f:ot", "f:at"), writes=("f:ot",))
            P.step("gp", (lambda ti=ti, tok0=tok0: (lambda g: g.dma_start(
                out=out_ap[tok0 + ti * 128:tok0 + (ti + 1) * 128, :], in_=otb[:])))(),
                reads=("f:ot",), writes=("OUT:all",), dma=True)


def ph_cast_stream(P, io, stack):
    s_ap, d_ap = io["L0_ffn_phi_w"], io["phi_bf"]
    for i in range(2):
        P.step("gp", (lambda i=i, s=s_ap, d=d_ap: (lambda g: g.dma_start(
            out=d[i * 1024:(i + 1) * 1024, :], in_=s[i * 1024:(i + 1) * 1024, :])))(),
            writes=(("phi_bf:all",) if i == 1 else ()), dma=True, dsem="cphi")
    mats = [
        ("L1_ffn_w1_w", "w1_bf", 4096, 1024),
        ("L1_ffn_w2_w", "w2_bf", 4096, 1024),
        ("L1_ffn_w3_w", "w3_bf", 1024, 4096),
        ("L1_ffn_h1_w", "h1_bf", 4096, 4096),
        ("L1_ffn_h2_w", "h2_bf", 4096, 4096),
    ]
    chunks = []
    for src, dst, rows, cols in mats:
        rstep = max(128, (1 << 20) // cols)
        for r0 in range(0, rows, rstep):
            chunks.append((src, dst, r0, min(r0 + rstep, rows)))
    for i, (src, dst, r0, r1) in enumerate(chunks):
        last = (i == len(chunks) - 1)
        P.step("gp", (lambda s=io[src], d=io[dst], r0=r0, r1=r1: (lambda g: g.dma_start(
            out=d[r0:r1, :], in_=s[r0:r1, :])))(),
            writes=(tuple(m[1] + ":all" for m in mats) if last else ()),
            dma=True, dsem="cffn")
    ew_s, ew_d = io["L0_ffn_expert_w"], io["ew_bf"]
    for n in range(NEXP):
        P.step("gp", (lambda n=n: (lambda g: g.dma_start(
            out=ew_d[n], in_=ew_s[n])))(),
            writes=(("ew_bf:all",) if n == NEXP - 1 else ()), dma=True, dsem="cew")

def ph_panel_ops(P, io, stack):
    NBG = 6
    tp = [sbuf(P, f"bg_tp{i}", [128, DH], BF16) for i in range(NBG)]
    panel_jobs = []
    specs = [
        ("w1_bf", "w1p", 4096, 1024),
        ("w2_bf", "w2p", 4096, 1024),
        ("h1_bf", "h1p", 4096, 4096),
        ("h2_bf", "h2p", 4096, 4096),
    ]
    for src_bf, dst_p, odim, idim in specs:
        s_ap, d_ap = io[src_bf], io[dst_p]
        nic = idim // 128
        for ic in range(nic):
            panel_jobs.append((
                src_bf, dst_p + ":all" if ic == nic - 1 else None,
                (lambda s=s_ap, ic=ic, odim=odim: (lambda t: (lambda e: e.dma_start(
                    out=t[:, 0:odim], in_=s[0:odim, ic * 128:(ic + 1) * 128],
                    transpose=True))))(),
                (lambda d=d_ap, ic=ic, odim=odim: (lambda t: (lambda e: e.dma_start(
                    out=d[ic], in_=t[:, 0:odim]))))(),
            ))
    s_ap, d_ap = io["w3_bf"], io["w3p"]
    for kc in range(DH // 128):
        panel_jobs.append((
            "w3_bf", "w3p:all" if kc == DH // 128 - 1 else None,
            (lambda s=s_ap, kc=kc: (lambda t: (lambda e: e.dma_start(
                out=t[:, 0:DM], in_=s[0:DM, kc * 128:(kc + 1) * 128],
                transpose=True))))(),
            (lambda d=d_ap, kc=kc: (lambda t: (lambda e: e.dma_start(
                out=d[kc * 128:(kc + 1) * 128, :], in_=t[:, 0:DM]))))(),
        ))

    def mk_tr(i):
        src_bf, _, tr_fn, _ = panel_jobs[i]
        return lambda: P.step("sync", tr_fn(tp[i % NBG]), reads=(src_bf + ":all",),
                              writes=(f"bgtp{i % NBG}",), dma=True)

    def mk_st(i):
        _, wkey, _, st_fn = panel_jobs[i]
        return lambda: P.step("sync", st_fn(tp[i % NBG]), reads=(f"bgtp{i % NBG}",),
                              writes=(wkey,) if wkey else (), dma=True)

    nj = len(panel_jobs)
    ops = []
    for i in range(min(NBG, nj)):
        ops.append(mk_tr(i))
    for i in range(nj):
        ops.append(mk_st(i))
        if i + NBG < nj:
            ops.append(mk_tr(i + NBG))
    return ops


def ph_panel(P, io, stack):
    for op in ph_panel_ops(P, io, stack):
        op()


def make_ln_bufs(P, name, G):
    return {
        "st": sbuf(P, f"lns_{name}", [128, G * 12], F32),
        "st2": sbuf(P, f"lna_{name}", [128, G * 2], F32),
        "rstd": sbuf(P, f"lnr_{name}", [128, G], F32),
        "nb": sbuf(P, f"lnb_{name}", [128, G], F32),
        "name": name, "G": G,
    }


def ln_tiles(P, bufs, src_slices, src_keys, dst_slices, dst_keys, eps=1e-5,
             act_func=AF.Copy, n_free=DM):
    G = len(src_slices)
    assert G <= bufs["G"]
    tag = bufs["name"]
    st, st2, rstd, nb = bufs["st"], bufs["st2"], bufs["rstd"], bufs["nb"]
    ks, ka, kr, kb = f"lns:{tag}", f"lna:{tag}", f"lnr:{tag}", f"lnb:{tag}"
    nhalf = n_free // 2
    for i, sl in enumerate(src_slices):
        P.step("dve", (lambda sl=sl, i=i: (lambda v: v.bn_stats(st[:, i * 12:i * 12 + 6], sl[:, 0:nhalf])))(),
               reads=(src_keys[i],), writes=(ks,))
        P.step("dve", (lambda sl=sl, i=i: (lambda v: v.bn_stats(st[:, i * 12 + 6:i * 12 + 12], sl[:, nhalf:n_free])))(),
               reads=(src_keys[i],), writes=(ks,))
    for i in range(G):
        P.step("dve", (lambda i=i: (lambda v: v.bn_aggr(st2[:, i * 2:i * 2 + 2], st[:, i * 12:i * 12 + 12])))(),
               reads=(ks,), writes=(ka,))
    var_view = st2[:].rearrange("p (g two) -> p g two", two=2)[:, :, 1:2].squeeze(2)
    mean_view = st2[:].rearrange("p (g two) -> p g two", two=2)[:, :, 0:1].squeeze(2)
    P.step("dve", (lambda: (lambda v: v.tensor_scalar(rstd[:], var_view, eps, None, ALU.add)))(),
           reads=(ka,), writes=(kr,))
    P.step("act", (lambda: (lambda a: a.activation(rstd[:], rstd[:], AF.Sqrt)))(),
           reads=(kr,), writes=(kr,))
    P.step("dve", (lambda: (lambda v: v.reciprocal(rstd[:], rstd[:])))(),
           reads=(kr,), writes=(kr,))
    if act_func == AF.Copy:
        P.step("dve", (lambda: (lambda v: v.tensor_scalar(nb[:], mean_view, -1.0, None, ALU.mult)))(),
               reads=(ka,), writes=(kb,))
        for i, (sl, dst) in enumerate(zip(src_slices, dst_slices)):
            P.step("dve", (lambda sl=sl, dst=dst, i=i: (lambda v: v.tensor_scalar(
                dst, sl, nb[:, i:i + 1], rstd[:, i:i + 1], ALU.add, ALU.mult)))(),
                reads=(src_keys[i], kr, kb), writes=(dst_keys[i],))
    else:
        P.step("dve", (lambda: (lambda v: v.tensor_tensor(nb[:], mean_view, rstd[:], ALU.mult)))(),
               reads=(ka, kr), writes=(kb,))
        P.step("dve", (lambda: (lambda v: v.tensor_scalar(nb[:], nb[:], -1.0, None, ALU.mult)))(),
               reads=(kb,), writes=(kb,))
        for i, (sl, dst) in enumerate(zip(src_slices, dst_slices)):
            P.step("act", (lambda sl=sl, dst=dst, i=i: (lambda a: a.activation(
                dst, sl, act_func, bias=nb[:, i:i + 1], scale=rstd[:, i:i + 1])))(),
                reads=(src_keys[i], kr, kb), writes=(dst_keys[i],))


def ph_lnA(P, io, stack, n_tok, x_name, out_name, tag, bgops=None):
    G = 8
    xb = sbuf(P, f"ln{tag}_x", [128, G * DM], F32)
    xn = sbuf(P, f"ln{tag}_o", [128, G * DM], BF16)
    lnb = make_ln_bufs(P, f"ln{tag}", G)
    x_ap, o_ap = io[x_name], io[out_name]
    ngroups = n_tok // (128 * G)
    bgops = list(bgops) if bgops else []
    bg_start = ngroups // 2
    per_g = ceil_div(len(bgops), max(1, ngroups - bg_start)) if bgops else 0
    bi = 0
    for g in range(ngroups):
        if g >= bg_start:
            for _ in range(per_g):
                if bi < len(bgops):
                    bgops[bi]()
                    bi += 1
        kx, ko = f"ln{tag}x", f"ln{tag}o"
        P.step("sync", (lambda g=g: (lambda e: e.dma_start(
            out=xb[:].rearrange("p (a f) -> p a f", a=G),
            in_=x_ap[g * G * 128:(g + 1) * G * 128, :].rearrange("(a p) f -> p a f", p=128))))(),
            writes=tuple(f"{kx}{i}" for i in range(G)), dma=True)
        src = [xb[:, i * DM:(i + 1) * DM] for i in range(G)]
        dst = [xn[:, i * DM:(i + 1) * DM] for i in range(G)]
        ln_tiles(P, lnb, src, [f"{kx}{i}" for i in range(G)],
                 dst, [f"{ko}{i}" for i in range(G)])
        P.step("sync", (lambda g=g: (lambda gp: gp.dma_start(
            out=o_ap[g * G * 128:(g + 1) * G * 128, :].rearrange("(a p) f -> p a f", p=128),
            in_=xn[:].rearrange("p (a f) -> p a f", a=G))))(),
            reads=tuple(f"{ko}{i}" for i in range(G)), writes=(f"{out_name}:all",), dma=True)
    while bi < len(bgops):
        bgops[bi]()
        bi += 1


def ph_logits(P, io, stack, local_lo, local_hi):
    nc = P.nc
    xn_ap = io["xn_tm"]
    phi_ap = io["phi_bf"]
    phib_ap = io["L0_ffn_phi_b"]
    ctm_ap = io["C_tm"]
    xt_ap = io["Xt_bf"]

    phiT = sbuf(P, "lg_phiT", [128, 8 * NE], BF16)
    for kc in range(8):
        P.step("sync", (lambda kc=kc: (lambda e: e.dma_start(
            out=phiT[:, kc * NE:(kc + 1) * NE],
            in_=phi_ap[0:NE, kc * 128:(kc + 1) * 128], transpose=True)))(),
            reads=("phi_bf:all",), writes=("lg:phiT",), dma=True)
    ones_f = sbuf(P, "lg_ones_f", [1, 128], F32)
    phib_row = sbuf(P, "lg_phib_row", [1, NE], F32)
    expb = sbuf(P, "lg_expb", [128, NE], BF16)
    lgA = psum(P, "lg_psA", [128, 1024])
    lgB = psum(P, "lg_psB", [128, 1024])
    xps = [psum(P, "lg_xpsA", [128, DM]), psum(P, "lg_xpsB", [128, DM])]
    P.step("sync", (lambda: (lambda e: e.dma_start(out=phib_row[:], in_=phib_ap[None, :])))(),
           writes=("lg:pbrow",), dma=True)
    P.step("dve", (lambda: (lambda v: v.memset(ones_f[:], 1.0)))(), writes=("lg:ones",))
    for j in range(2):
        ps_j = (lgA, lgB)[j]
        P.step("pe", (lambda j=j, ps_j=ps_j: (lambda t: (
            t.matmul(ps_j[:, 0:512], ones_f[:], phib_row[:, j * 1024:j * 1024 + 512],
                     start=True, stop=True),
            t.matmul(ps_j[:, 512:1024], ones_f[:], phib_row[:, j * 1024 + 512:(j + 1) * 1024],
                     start=True, stop=True))[-1]))(),
            reads=("lg:pbrow", "lg:ones"), writes=(f"ps:lg{j}",))
        P.step("act", (lambda j=j, ps_j=ps_j: (lambda a: a.activation(
            expb[:, j * 1024:(j + 1) * 1024], ps_j[:], AF.Exp)))(),
            reads=(f"ps:lg{j}",), writes=("lg:expb",))

    fm = sbuf(P, "lg_fm", [128, 8 * 1024], BF16)
    tm2 = sbuf(P, "lg_tm", [128, 8 * DM], BF16)
    Dbuf = sbuf(P, "lg_D", [128, 8 * NE], BF16)
    Ebuf2 = sbuf(P, "lg_E", [128, 2 * NE], BF16)
    Cbuf2 = sbuf(P, "lg_C", [128, 2 * NE], BF16)
    sD = sbuf(P, "lg_sD", [128, 2 * 64], F32)
    rD = sbuf(P, "lg_rD", [128, 2 * 64], F32)
    sC = sbuf(P, "lg_sC", [128, 2 * 32], F32)
    rC = sbuf(P, "lg_rC", [128, 2 * 32], F32)
    Xacc = sbuf(P, "lg_Xacc", [128, 16 * DM], F32)

    n_half = TA // 1024
    for hs in range(n_half):
        t0 = hs * 8
        tm = tm2[:, :]
        ktm = "lg:tm"
        P.step("sync", (lambda hs=hs, tm=tm: (lambda e: e.dma_start(
            out=tm.rearrange("p (a f) -> p a f", a=8),
            in_=xn_ap[hs * 1024:(hs + 1) * 1024, :].rearrange("(a p) f -> p a f", p=128))))(),
            reads=("xn_tm:all",), writes=(ktm,), dma=True)
        for kc in range(8):
            P.step("sync", (lambda hs=hs, kc=kc: (lambda e: e.dma_start(
                out=fm[:, kc * 1024:(kc + 1) * 1024],
                in_=xn_ap[hs * 1024:(hs + 1) * 1024, kc * 128:(kc + 1) * 128],
                transpose=True)))(),
                reads=("xn_tm:all",), writes=("lg:fm",), dma=True)
        for ti in range(8):
            t = t0 + ti
            tp_ = ti % 2
            Ebuf = Ebuf2[:, tp_ * NE:(tp_ + 1) * NE]
            Cbuf = Cbuf2[:, tp_ * NE:(tp_ + 1) * NE]
            sDv, rDv = sD[:, tp_ * 64:(tp_ + 1) * 64], rD[:, tp_ * 64:(tp_ + 1) * 64]
            sCv, rCv = sC[:, tp_ * 32:(tp_ + 1) * 32], rC[:, tp_ * 32:(tp_ + 1) * 32]
            kE, kC = f"lg:E{tp_}", f"lg:C{tp_}"
            ksD, krD = f"lg:sD{tp_}", f"lg:rD{tp_}"
            ksC, krC = f"lg:sC{tp_}", f"lg:rC{tp_}"
            for j, ps_j in ((0, lgA), (1, lgB)):
                pairs = [(fm[:, kc * 1024 + ti * 128: kc * 1024 + (ti + 1) * 128],
                          phiT[:, kc * NE + j * 1024: kc * NE + (j + 1) * 1024])
                         for kc in range(8)]
                mm_group(P, ps_j[:], pairs, f"ps:lg{j}", ("lg:fm", "lg:phiT"))
                P.step("act", (lambda j=j, ps_j=ps_j, Ebuf=Ebuf: (lambda a: a.activation(
                    Ebuf[:, j * 1024:(j + 1) * 1024], ps_j[:], AF.Exp)))(),
                    reads=(f"ps:lg{j}",), writes=(kE,))
            P.step("dve", (lambda Ebuf=Ebuf: (lambda v: v.tensor_mul(Ebuf, Ebuf, expb[:])))(),
                   reads=(kE, "lg:expb"), writes=(kE,))
            P.step("dve", (lambda Ebuf=Ebuf, sDv=sDv: (lambda v: v.reduce_sum(
                sDv, Ebuf.rearrange("p (g k) -> p g k", k=32), axis=AX.X)))(),
                reads=(kE,), writes=(ksD,))
            P.step("dve", (lambda sDv=sDv, rDv=rDv: (lambda v: v.reciprocal(rDv, sDv)))(),
                   reads=(ksD,), writes=(krD,))
            P.step("dve", (lambda ti=ti, Ebuf=Ebuf, rDv=rDv: (lambda v: v.tensor_tensor(
                Dbuf[:, ti * NE:(ti + 1) * NE].rearrange("p (g k) -> p g k", k=32),
                Ebuf.rearrange("p (g k) -> p g k", k=32),
                rDv.unsqueeze(2).broadcast_to([128, 64, 32]),
                ALU.mult)))(),
                reads=(kE, krD), writes=("lg:D",))
            if local_lo <= t * 128 < local_hi:
                P.step("dve", (lambda Ebuf=Ebuf, sCv=sCv: (lambda v: v.reduce_sum(
                    sCv, Ebuf.rearrange("p (g k) -> p k g", k=32), axis=AX.X)))(),
                    reads=(kE,), writes=(ksC,))
                P.step("dve", (lambda sCv=sCv, rCv=rCv: (lambda v: v.reciprocal(rCv, sCv)))(),
                       reads=(ksC,), writes=(krC,))
                P.step("dve", (lambda Ebuf=Ebuf, Cbuf=Cbuf, rCv=rCv: (lambda v: v.tensor_tensor(
                    Cbuf.rearrange("p (k g) -> p k g", k=32),
                    Ebuf.rearrange("p (g k) -> p k g", k=32),
                    rCv.unsqueeze(2).broadcast_to([128, 32, 64]),
                    ALU.mult)))(),
                    reads=(kE, krC), writes=(kC,))
                tloc = t * 128 - local_lo
                P.step("sync", (lambda tloc=tloc, Cbuf=Cbuf: (lambda g: g.dma_start(
                    out=ctm_ap[tloc:tloc + 128, :], in_=Cbuf)))(),
                    reads=(kC,), writes=("C_tm:all",), dma=True)
        for ec in range(16):
            xp = xps[ec % 2]
            kps = f"ps:lgx{ec % 2}"
            pairs = [(Dbuf[:, ti * NE + ec * 128: ti * NE + (ec + 1) * 128],
                      tm.rearrange("p (a f) -> p a f", a=8)[:, ti, :])
                     for ti in range(8)]
            mm_group(P, xp[:], pairs, kps, ("lg:D", ktm))
            if hs == 0:
                P.step("dve", (lambda ec=ec, xp=xp: (lambda v: v.tensor_copy(
                    Xacc[:, ec * DM:(ec + 1) * DM], xp[:])))(),
                    reads=(kps,), writes=(f"lg:Xacc{ec}",))
            else:
                P.step("dve", (lambda ec=ec, xp=xp: (lambda v: v.tensor_add(
                    Xacc[:, ec * DM:(ec + 1) * DM],
                    Xacc[:, ec * DM:(ec + 1) * DM], xp[:])))(),
                    reads=(kps, f"lg:Xacc{ec}"), writes=(f"lg:Xacc{ec}",))

    xt_o = sbuf(P, "lg_xto", [128, 4 * DM], BF16)
    lnb = make_ln_bufs(P, "lgx", 4)
    for eg in range(4):
        for i in range(4):
            ec = eg * 4 + i
            kx = f"lg:Xacc{ec}"
            sl = Xacc[:, ec * DM:(ec + 1) * DM]
            P.step("dve", (lambda sl=sl: (lambda v: v.tensor_scalar(
                sl, sl, -33000.0, 65000.0, ALU.max, ALU.min)))(),
                reads=(kx,), writes=(kx,))
        src = [Xacc[:, (eg * 4 + i) * DM:(eg * 4 + i + 1) * DM] for i in range(4)]
        dst = [xt_o[:, i * DM:(i + 1) * DM] for i in range(4)]
        ln_tiles(P, lnb, src, [f"lg:Xacc{eg * 4 + i}" for i in range(4)],
                 dst, [f"lg:xto{i}" for i in range(4)], act_func=AF.Tanh)
        P.step("sync", (lambda eg=eg: (lambda g: g.dma_start(
            out=xt_ap[eg * 512:(eg + 1) * 512, :].rearrange("(a p) f -> p a f", p=128),
            in_=xt_o[:].rearrange("p (a f) -> p a f", a=4))))(),
            reads=tuple(f"lg:xto{i}" for i in range(4)), writes=("Xt_bf:all",), dma=True)



INPUT_SPECS = {
    "L0_attn_w_qkv": ((DM, 3 * DM), F32),
    "L0_attn_wo": ((DM, DM), F32),
    "L0_attn_lambda_q1": ((HD0,), F32),
    "L0_attn_lambda_q2": ((HD0,), F32),
    "L0_attn_lambda_k1": ((HD0,), F32),
    "L0_attn_lambda_k2": ((HD0,), F32),
    "L0_attn_norm_w": ((H0, HD0), F32),
    "L0_ffn_phi_w": ((NE, DM), F32),
    "L0_ffn_phi_b": ((NE,), F32),
    "L0_ffn_expert_w": ((NEXP, DM, DM), F32),
    "L0_ffn_expert_b": ((NEXP, DM), F32),
    "L1_attn_w_qkv": ((DM, 6 * DM), F32),
    "L1_attn_w_out": ((DM, DM), F32),
    "L1_ffn_w1_w": ((DH, DM), F32),
    "L1_ffn_w1_s": ((DH,), F32),
    "L1_ffn_w2_w": ((DH, DM), F32),
    "L1_ffn_w2_s": ((DH,), F32),
    "L1_ffn_h1_w": ((DH, DH), F32),
    "L1_ffn_h1_s": ((DH,), F32),
    "L1_ffn_h2_w": ((DH, DH), F32),
    "L1_ffn_h2_s": ((DH,), F32),
    "L1_ffn_w3_w": ((DM, DH), F32),
    "L1_ffn_w3_s": ((DM,), F32),
    "x_full": ((TA, DM), F32),
    "x_loc": ((TL, DM), F32),
    "IDT": ((128, 128), BF16),
    "IDTF": ((128, 128), F32),
    "MSL": ((128, 128), F32),
    "MLE": ((128, 128), BF16),
}

INTERM_SPECS = {
    "xn_tm": ((TA, DM), BF16),
    "phi_bf": ((NE, DM), BF16),
    "ew_bf": ((NEXP, DM, DM), BF16),
    "w1_bf": ((DH, DM), BF16),
    "w2_bf": ((DH, DM), BF16),
    "h1_bf": ((DH, DH), BF16),
    "h2_bf": ((DH, DH), BF16),
    "w3_bf": ((DM, DH), BF16),
    "w1p": ((8, 128, DH), BF16),
    "w2p": ((8, 128, DH), BF16),
    "h1p": ((32, 128, DH), BF16),
    "h2p": ((32, 128, DH), BF16),
    "w3p": ((DH, DM), BF16),
    "C_tm": ((TL, NE), BF16),
    "Xt_bf": ((NE, DM), BF16),
    "Yt_bf": ((NE, DM), BF16),
    "qkv0_fm": ((3 * DM, TL), BF16),
    "attn0": ((TL, DM), F32),
    "Cp": ((16, 128, TL), BF16),
    "x1": ((TL, DM), F32),
    "xn1_tm": ((TL, DM), BF16),
    "qkv6_fm": ((6 * DM, TL), BF16),
    "attn1": ((TL, DM), F32),
}

PHASES = {
    "cast": (lambda P, io, st: ph_cast_stream(P, io, st),
             ["L0_ffn_phi_w", "L0_ffn_expert_w", "L1_ffn_w1_w",
              "L1_ffn_w2_w", "L1_ffn_w3_w", "L1_ffn_h1_w", "L1_ffn_h2_w"],
             [], ["phi_bf", "ew_bf", "w1_bf", "w2_bf", "w3_bf",
                  "h1_bf", "h2_bf"]),
    "ln": (lambda P, io, st: ph_lnA(P, io, st, TA, "x_full", "xn_tm", "A",
                                    bgops=ph_panel_ops(P, io, st)),
           ["x_full"], ["w1_bf", "w2_bf", "h1_bf", "h2_bf", "w3_bf"],
           ["xn_tm", "w1p", "w2p", "h1p", "h2p", "w3p"]),
    "logits": (lambda P, io, st: ph_logits(P, io, st, 0, TL),
               ["L0_ffn_phi_b"], ["xn_tm", "phi_bf"], ["C_tm", "Xt_bf"]),
    "qkv0": (lambda P, io, st: ph_qkv0(P, io, st),
             ["L0_attn_w_qkv"], ["xn_tm"], ["qkv0_fm"]),
    "attn0": (lambda P, io, st: ph_attn0(P, io, st),
              ["L0_attn_lambda_q1", "L0_attn_lambda_q2",
               "L0_attn_lambda_k1", "L0_attn_lambda_k2", "L0_attn_norm_w",
               "L0_attn_wo", "IDT", "IDTF"],
              ["qkv0_fm"], ["attn0"]),
    "experts": (lambda P, io, st: ph_experts(P, io, st),
                ["L0_ffn_expert_b"], ["Xt_bf", "ew_bf"], ["Yt_bf"]),
    "ctrans": (lambda P, io, st: ph_ctrans(P, io, st),
               ["IDT"], ["C_tm"], ["Cp"]),
    "y": (lambda P, io, st: ph_y(P, io, st),
          ["x_loc"], ["Cp", "Yt_bf", "attn0"], ["x1", "xn1_tm"]),
    "qkv1": (lambda P, io, st: ph_qkv1(P, io, st),
             ["L1_attn_w_qkv"], ["xn1_tm"], ["qkv6_fm"]),
    "attn1": (lambda P, io, st: ph_attn1(P, io, st),
              ["MSL", "MLE", "L1_attn_w_out"], ["qkv6_fm"], ["attn1"]),
    "ffn1": (lambda P, io, st: ph_ffn1(P, io, st),
             ["L1_ffn_w1_s", "L1_ffn_w2_s", "L1_ffn_h1_s", "L1_ffn_h2_s",
              "L1_ffn_w3_s", "IDTF"],
             ["xn1_tm", "x1", "attn1", "w1p", "w2p", "h1p", "h2p", "w3p"],
             []),
}

PHASE_ORDER = ["cast", "ln", "logits", "qkv0", "attn0",
               "experts", "ctrans", "y", "qkv1", "attn1", "ffn1"]
NO_BARRIER = {"cast"}
OUTER_SCOPE = set()

PHASE_GROUPS = [["cast", "ln"], ["logits"], ["qkv0"], ["attn0"],
                ["experts", "ctrans"], ["y"], ["qkv1"], ["attn1"], ["ffn1"]]


def _add_io(nc, io, name, kind):
    if name in io:
        return
    shape, dt = INPUT_SPECS.get(name) or INTERM_SPECS[name]
    io[name] = nc.dram_tensor(name, list(shape), dt, kind=kind).ap()


def build_group(group):
    nc = bass.Bass("TRN2", target_bir_lowering=False)
    io = {}
    ins, prods, cons = set(), set(), set()
    for ph in group:
        _, in_names, consumed, produced = PHASES[ph]
        ins.update(in_names)
        cons.update(consumed)
        prods.update(produced)
    ext_in = sorted(ins) + sorted(cons - prods)
    last = (group[-1] == "ffn1")
    later_needs = set()
    gi = PHASE_GROUPS.index(group) if group in PHASE_GROUPS else None
    if gi is not None:
        for g2 in PHASE_GROUPS[gi + 1:]:
            for ph in g2:
                later_needs.update(PHASES[ph][2])
    ext_out = sorted(prods & later_needs)
    internal = sorted(prods - later_needs)
    for n in ext_in:
        _add_io(nc, io, n, "ExternalInput")
    for n in ext_out:
        _add_io(nc, io, n, "ExternalOutput")
    for n in internal:
        _add_io(nc, io, n, "Internal")
    if last:
        io["OUT"] = nc.dram_tensor("OUT", [TL, DM], F32, kind="ExternalOutput").ap()
        ext_out = ext_out + ["OUT"]
    io["xnloc_tm"] = io["xn_tm"][0:TL, :] if "xn_tm" in io else None
    with ExitStack() as stack:
        P = Prog(nc, stack)
        for ph in group:
            PHASES[ph][0](P, io, stack)
            P.barrier(full=True)
        block = stack.enter_context(nc.Block())
        P.finalize(block)
    return nc, ext_in, ext_out


def build_fused():
    nc = bass.Bass("TRN2", target_bir_lowering=False)
    io = {}
    for n in INPUT_SPECS:
        _add_io(nc, io, n, "ExternalInput")
    for n in INTERM_SPECS:
        _add_io(nc, io, n, "Internal")
    io["OUT"] = nc.dram_tensor("OUT", [TL, DM], F32, kind="ExternalOutput").ap()
    io["xnloc_tm"] = io["xn_tm"][0:TL, :]
    with ExitStack() as stack:
        P = Prog(nc, stack)
        for ph in PHASE_ORDER:
            if ph in OUTER_SCOPE:
                P.stack = stack
                PHASES[ph][0](P, io, stack)
            else:
                with ExitStack() as phase_stack:
                    P.stack = phase_stack
                    PHASES[ph][0](P, io, phase_stack)
                    if ph not in NO_BARRIER:
                        P.barrier()
        P.stack = stack
        block = stack.enter_context(nc.Block())
        P.finalize(block)
    return nc


def host_consts():
    idt = np.eye(128, dtype=ml_dtypes.bfloat16)
    idtf = np.eye(128, dtype=np.float32)
    c = np.arange(128)
    msl = (c[:, None] > c[None, :]).astype(np.float32)
    mle = (c[:, None] <= c[None, :]).astype(ml_dtypes.bfloat16)
    return {"IDT": idt, "IDTF": idtf, "MSL": msl, "MLE": mle}


FUSED = True


def kernel(**inputs):
    x = np.ascontiguousarray(np.asarray(inputs["L0_attn_x"], dtype=np.float32))
    consts = host_consts()
    base = {k: np.ascontiguousarray(np.asarray(v)) for k, v in inputs.items()
            if k != "L0_attn_x"}
    per_core = []
    for c in range(NCORES):
        order = [2 * c, 2 * c + 1] + [b for b in range(B) if b not in (2 * c, 2 * c + 1)]
        xf = x[order].reshape(TA, DM)
        xl = x[2 * c:2 * c + 2].reshape(TL, DM)
        m = dict(base)
        m.update(consts)
        m["x_full"] = xf
        m["x_loc"] = xl
        per_core.append(m)

    if FUSED:
        nc = build_fused()
        res = run_bass_kernel_spmd(
            nc, [{k: m[k] for k in INPUT_SPECS} for m in per_core],
            core_ids=list(range(NCORES)))
        outs = [r["OUT"] for r in res.results]
    else:
        state = [dict() for _ in range(NCORES)]
        outs = None
        for group in PHASE_GROUPS:
            nc, ext_in, ext_out = build_group(group)
            in_maps = []
            for c in range(NCORES):
                m = {}
                for n in ext_in:
                    m[n] = per_core[c][n] if n in per_core[c] else state[c][n]
                in_maps.append(m)
            res = run_bass_kernel_spmd(nc, in_maps, core_ids=list(range(NCORES)))
            for c in range(NCORES):
                for n in ext_out:
                    state[c][n] = res.results[c][n]
            if "OUT" in ext_out:
                outs = [state[c]["OUT"] for c in range(NCORES)]
    full = np.concatenate([np.asarray(o, dtype=np.float32)[None] for o in outs], axis=0)
    return full.reshape(B, S, DM)


# version 16
# speedup vs baseline: 1.2294x; 1.2294x over previous
import numpy as np
import ml_dtypes
from contextlib import ExitStack
import concourse.bass as bass
import concourse.mybir as mybir
from concourse.bass_utils import run_bass_kernel_spmd

F32 = mybir.dt.float32
BF16 = mybir.dt.bfloat16
AF = mybir.ActivationFunctionType
ALU = mybir.AluOpType
AX = mybir.AxisListType

B = 16
BL = 2
S = 2048
DM = 1024
TA = B * S
TL = BL * S
NE = 2048
NEXP = 64
PSLOT = 32
H0, HD0 = 16, 64
H1, HD1 = 8, 128
DH = 4096
LAMBDA_INIT = 0.8
SCALE0 = 1.0 / np.sqrt(64.0)
SCALE1 = 128.0 ** (-0.5)
NCORES = 8

ENGINES = ("pe", "act", "dve", "sync", "gp")


NSLOT = 8


class Prog:

    CAST_SEMS = ("cphi", "cew", "cffn")

    def __init__(self, nc, stack):
        self.nc = nc
        self.stack = stack
        self.semids = ["pe", "act", "dve"] + \
            [f"sync{i}" for i in range(NSLOT)] + [f"gp{i}" for i in range(NSLOT)] + \
            [f"actd{i}" for i in range(NSLOT)] + list(self.CAST_SEMS)
        self.sems = {s: stack.enter_context(nc.semaphore(f"sem_{s}")) for s in self.semids}
        self.ops = {e: [] for e in ENGINES}
        self.semcnt = {s: 0 for s in self.semids}
        self.dma_i = {"sync": 0, "gp": 0, "act": 0}
        self.clock = {e: {s: 0 for s in self.semids} for e in ENGINES}
        self.snaps = {s: [] for s in self.semids}
        self.writer = {}
        self.readers = {}
        self.nwaits = 0
        self.bg = []
        self.bg_i = 0

    def drain_bg(self, k):
        while k > 0 and self.bg_i < len(self.bg):
            self.bg[self.bg_i]()
            self.bg_i += 1
            k -= 1

    def _merge(self, eng, semid, val):
        snaps = self.snaps[semid]
        best = None
        lo, hi = 0, len(snaps)
        while lo < hi:
            mid = (lo + hi) // 2
            if snaps[mid][0] <= val:
                lo = mid + 1
            else:
                hi = mid
        if lo > 0:
            best = snaps[lo - 1][1]
        ck = self.clock[eng]
        if best is not None:
            for k, v in best.items():
                if v > ck.get(k, 0):
                    ck[k] = v
        if val > ck.get(semid, 0):
            ck[semid] = val

    def _wait(self, eng, semid, val):
        if self.clock[eng].get(semid, 0) >= val:
            return
        self.ops[eng].append(("w", semid, val))
        self.nwaits += 1
        self._merge(eng, semid, val)

    def step(self, eng, fn, reads=(), writes=(), dma=False, tag=None, dsem=None,
             pe_fence=False):
        if pe_fence and eng == "pe" and self.semcnt["pe"] > 0:
            self.ops["pe"].append(("w", "pe", self.semcnt["pe"]))
            self.nwaits += 1
            self._merge("pe", "pe", self.semcnt["pe"])
        need = {}
        for k in reads:
            w = self.writer.get(k)
            if w is not None:
                need[w[0]] = max(need.get(w[0], 0), w[1])
        for k in writes:
            w = self.writer.get(k)
            if w is not None:
                need[w[0]] = max(need.get(w[0], 0), w[1])
            for s2, c2 in self.readers.get(k, {}).items():
                need[s2] = max(need.get(s2, 0), c2)
        for s2, v2 in need.items():
            self._wait(eng, s2, v2)
        if dsem is not None:
            assert dma
            semid = dsem
            inc = 16
        elif dma:
            assert eng in ("sync", "gp", "act")
            slot = self.dma_i[eng] % NSLOT
            self.dma_i[eng] += 1
            semid = f"{eng}{slot}" if eng != "act" else f"actd{slot}"
            if self.semcnt[semid] > 0:
                self._wait(eng, semid, self.semcnt[semid])
            inc = 16
        else:
            semid = eng
            inc = 1
        self.semcnt[semid] += inc
        mycount = self.semcnt[semid]
        self.ops[eng].append(("op", fn, semid, inc))
        if eng == "pe" and not dma:
            self.clock["pe"]["pe"] = mycount
        snap = dict(self.clock[eng])
        snap[semid] = mycount
        self.snaps[semid].append((mycount, snap))
        for k in writes:
            self.writer[k] = (semid, mycount)
            self.readers[k] = {}
        for k in reads:
            self.readers.setdefault(k, {})
            cur = self.readers[k].get(semid, 0)
            self.readers[k][semid] = max(cur, mycount)
        return semid, mycount

    def barrier(self, full=False):
        for e in ENGINES:
            for s in self.semids:
                if s in self.CAST_SEMS and not full:
                    continue
                if self.semcnt[s] > 0:
                    self._wait(e, s, self.semcnt[s])

    def finalize(self, block):
        sems = self.sems

        def make_runner(eng):
            items = self.ops[eng]

            def run(engobj):
                for it in items:
                    if it[0] == "w":
                        engobj.wait_ge(sems[it[1]], it[2])
                    else:
                        last = it[1](engobj)
                        last.then_inc(sems[it[2]], it[3])
            return run

        block.tensor(make_runner("pe"))
        block.scalar(make_runner("act"))
        block.vector(make_runner("dve"))
        block.sync(make_runner("sync"))
        block.gpsimd(make_runner("gp"))


def ceil_div(a, b):
    return (a + b - 1) // b


def mm_group(P, psum_ap, pairs, psum_key, read_keys, start=True, stop=True,
             nmax=512):
    ncols = psum_ap.shape[-1]
    nsl = ceil_div(ncols, nmax)

    def fn(t):
        last = None
        for i, (l, r) in enumerate(pairs):
            for j in range(nsl):
                c0, c1 = j * nmax, min((j + 1) * nmax, ncols)
                last = t.matmul(psum_ap[:, c0:c1], l, r[:, c0:c1],
                                start=(start and i == 0), stop=(stop and i == len(pairs) - 1))
        return last
    P.step("pe", fn, reads=tuple(read_keys), writes=(psum_key,))



def sbuf(P, name, shape, dtype):
    return P.stack.enter_context(P.nc.sbuf_tensor(name, shape, dtype))


def psum(P, name, shape):
    return P.stack.enter_context(P.nc.psum_tensor(name, shape, F32))


DT_NP = {F32: np.float32, BF16: ml_dtypes.bfloat16}


def ph_qkv0(P, io, stack):
    w_ap = io["L0_attn_w_qkv"]
    xn_ap = io["xnloc_tm"]
    out_ap = io["qkv0_fm"]
    NF = 3 * DM
    w_sb = sbuf(P, "q0_w", [128, 8 * NF], BF16)
    for dc in range(8):
        P.step("gp", (lambda dc=dc: (lambda g: g.dma_start(
            out=w_sb[:, dc * NF:(dc + 1) * NF], in_=w_ap[dc * 128:(dc + 1) * 128, :])))(),
            writes=("q0:w",), dma=True)
    fm = sbuf(P, "q0_fm", [128, 8 * TL], BF16)
    for dc in range(8):
        P.step("sync", (lambda dc=dc: (lambda e: e.dma_start(
            out=fm[:, dc * TL:(dc + 1) * TL],
            in_=xn_ap[0:TL, dc * 128:(dc + 1) * 128], transpose=True)))(),
            reads=("xnloc_tm:all",), writes=("q0:fm",), dma=True)
    ps = [psum(P, "q0_psA", [128, 1024]), psum(P, "q0_psB", [128, 1024])]
    ev = sbuf(P, "q0_ev", [128, 2 * 1024], BF16)
    i = 0
    for fc in range(NF // 128):
        for tq in range(TL // 1024):
            pi = i % 2
            i += 1
            pairs = [(w_sb[:, dc * NF + fc * 128:dc * NF + (fc + 1) * 128],
                      fm[:, dc * TL + tq * 1024:dc * TL + (tq + 1) * 1024])
                     for dc in range(8)]
            mm_group(P, ps[pi][:], pairs, f"ps:q0{pi}", ("q0:w", "q0:fm"))
            P.step("act", (lambda pi=pi: (lambda a: a.activation(
                ev[:, pi * 1024:(pi + 1) * 1024], ps[pi][:], AF.Copy)))(),
                reads=(f"ps:q0{pi}",), writes=(f"q0:ev{pi}",))
            P.step("gp", (lambda fc=fc, tq=tq, pi=pi: (lambda g: g.dma_start(
                out=out_ap[fc * 128:(fc + 1) * 128, tq * 1024:(tq + 1) * 1024],
                in_=ev[:, pi * 1024:(pi + 1) * 1024])))(),
                reads=(f"q0:ev{pi}",), writes=("qkv0_fm:all",), dma=True)


def ph_attn0(P, io, stack):
    qkv_ap = io["qkv0_fm"]
    wo_ap = io["L0_attn_wo"]
    out_ap = io["attn0"]
    NKT = S // 128
    QH = S // 2
    idt = sbuf(P, "a0_idt", [128, 128], BF16)
    idtf = sbuf(P, "a0_idtf", [128, 128], F32)
    P.step("sync", (lambda: (lambda e: e.dma_start(out=idt[:], in_=io["IDT"][:])))(),
           writes=("a0:idt",), dma=True)
    P.step("sync", (lambda: (lambda e: e.dma_start(out=idtf[:], in_=io["IDTF"][:])))(),
           writes=("a0:idtf",), dma=True)
    ones_f = sbuf(P, "a0_onesf", [1, 128], F32)
    ones_c = sbuf(P, "a0_onesc", [128, 1], BF16)
    P.step("dve", (lambda: (lambda v: v.memset(ones_f[:], 1.0)))(), writes=("a0:onesf",))
    P.step("dve", (lambda: (lambda v: v.memset(ones_c[:], 1.0)))(), writes=("a0:onesc",))
    lamv = sbuf(P, "a0_lamv", [1, 4 * 64], F32)
    for i, nm in enumerate(["L0_attn_lambda_q1", "L0_attn_lambda_q2",
                            "L0_attn_lambda_k1", "L0_attn_lambda_k2"]):
        P.step("sync", (lambda i=i, nm=nm: (lambda e: e.dma_start(
            out=lamv[:, i * 64:(i + 1) * 64], in_=io[nm][None, :])))(),
            writes=("a0:lamv",), dma=True)
    prod = sbuf(P, "a0_prod", [1, 128], F32)
    dots = sbuf(P, "a0_dots", [1, 2], F32)
    expd = sbuf(P, "a0_expd", [1, 2], F32)
    nlam1 = sbuf(P, "a0_nlam1", [1, 1], F32)
    P.step("dve", (lambda: (lambda v: v.tensor_mul(
        prod[:].rearrange("o (a f) -> o a f", a=2),
        lamv[:, 0:128].rearrange("o (a f) -> o a f", a=2),
        lamv[:, 128:256].rearrange("o (a f) -> o a f", a=2))))(),
        reads=("a0:lamv",), writes=("a0:prod",))
    P.step("dve", (lambda: (lambda v: v.reduce_sum(
        dots[:], prod[:].rearrange("o (a f) -> o a f", a=2), axis=AX.X)))(),
        reads=("a0:prod",), writes=("a0:dots",))
    P.step("act", (lambda: (lambda a: a.activation(expd[:], dots[:], AF.Exp)))(),
           reads=("a0:dots",), writes=("a0:expd",))
    P.step("dve", (lambda: (lambda v: v.tensor_sub(nlam1[:], expd[:, 1:2], expd[:, 0:1])))(),
           reads=("a0:expd",), writes=("a0:nlam",))
    P.step("dve", (lambda: (lambda v: v.tensor_scalar(nlam1[:], nlam1[:], -LAMBDA_INIT, None, ALU.add)))(),
           reads=("a0:nlam",), writes=("a0:nlam",))
    nlam128 = sbuf(P, "a0_nlamb", [128, 1], F32)
    psQK = psum(P, "a0_psQK", [128, 2048])
    psV = psum(P, "a0_psV", [128, 1024])
    psT = psum(P, "a0_psT", [128, 1024])
    P.step("pe", (lambda: (lambda t: t.matmul(psT[0:128, 0:1], ones_f[:, 0:128], nlam1[:],
                                              start=True, stop=True)))(),
           reads=("a0:nlam", "a0:onesf"), writes=("ps:a0T",))
    P.step("act", (lambda: (lambda a: a.activation(nlam128[:], psT[0:128, 0:1], AF.Copy)))(),
           reads=("ps:a0T",), writes=("a0:nlamb",))
    nwrow = sbuf(P, "a0_nwrow", [1, H0 * HD0], F32)
    nwb = sbuf(P, "a0_nwb", [128, H0 * HD0], F32)
    P.step("sync", (lambda: (lambda e: e.dma_start(
        out=nwrow[:], in_=io["L0_attn_norm_w"][:].rearrange("a b -> (a b)")[None, :])))(),
        writes=("a0:nw",), dma=True)
    P.step("pe", (lambda: (lambda t: (
        t.matmul(psT[:, 0:512], ones_f[:, 0:128], nwrow[:, 0:512], start=True, stop=True),
        t.matmul(psT[:, 512:1024], ones_f[:, 0:128], nwrow[:, 512:1024], start=True, stop=True))[-1]))(),
        reads=("a0:nw", "a0:onesf"), writes=("ps:a0T",))
    P.step("act", (lambda: (lambda a: a.activation(nwb[:], psT[:, 0:1024], AF.Copy,
                                                   scale=1.0 - LAMBDA_INIT)))(),
           reads=("ps:a0T",), writes=("a0:nwb",))
    wo_sb = sbuf(P, "a0_wo", [128, 8 * DM], BF16)
    for c in range(8):
        P.step("gp", (lambda c=c: (lambda g: g.dma_start(
            out=wo_sb[:, c * DM:(c + 1) * DM], in_=wo_ap[c * 128:(c + 1) * 128, :])))(),
            writes=("a0:wo",), dma=True)

    q_sb = sbuf(P, "a0_q", [128, S], BF16)
    k_sb = sbuf(P, "a0_k", [128, S], BF16)
    vt_b = sbuf(P, "a0_vt", [128, NKT * DM], BF16)
    vb65 = sbuf(P, "a0_vb65", [128, NKT * 65], BF16)
    Eb = sbuf(P, "a0_E", [128, 32 * 512], BF16)
    o1 = sbuf(P, "a0_o1", [65, 512], F32)
    o2 = sbuf(P, "a0_o2", [65, 512], F32)
    o1T = sbuf(P, "a0_o1T", [128, 4 * 65], F32)
    o2T = sbuf(P, "a0_o2T", [128, 4 * 65], F32)
    rs1T = sbuf(P, "a0_rs1T", [128, 4], F32)
    rs2T = sbuf(P, "a0_rs2T", [128, 4], F32)
    t1T = sbuf(P, "a0_t1T", [128, 4 * 64], F32)
    osbT = sbuf(P, "a0_osbT", [128, 4 * 64], F32)
    sqT = sbuf(P, "a0_sqT", [128, 4 * 64], F32)
    msT = sbuf(P, "a0_msT", [128, 4], F32)
    rstdT = sbuf(P, "a0_rstdT", [128, 4], F32)
    onT = sbuf(P, "a0_onT", [128, 4 * 64], F32)
    oft = sbuf(P, "a0_oft", [128, 8 * S], BF16)
    evt = sbuf(P, "a0_evt", [128, DM], F32)
    P.step("dve", (lambda: (lambda v: v.memset(
        vb65[:].rearrange("p (kt f) -> p kt f", f=65)[:, :, 64:65], 1.0)))(),
        writes=("a0:vb65c",))

    for b in range(BL):
        for kt in range(NKT):
            P.step("sync", (lambda b=b, kt=kt: (lambda e: e.dma_start(
                out=vt_b[:, kt * DM:(kt + 1) * DM],
                in_=qkv_ap[2 * DM:3 * DM, b * S + kt * 128:b * S + (kt + 1) * 128],
                transpose=True)))(),
                reads=("qkv0_fm:all",), writes=("a0:vt",), dma=True)
        for h in range(H0):
            for half in range(2):
                P.step("sync", (lambda b=b, h=h, half=half: (lambda e: e.dma_start(
                    out=q_sb[64 * half:64 * (half + 1), :],
                    in_=qkv_ap[h * 64:(h + 1) * 64, b * S:(b + 1) * S])))(),
                    reads=("qkv0_fm:all",), writes=("a0:q",), dma=True)
                P.step("sync", (lambda b=b, h=h, half=half: (lambda e: e.dma_start(
                    out=k_sb[64 * half:64 * (half + 1), :],
                    in_=qkv_ap[DM + h * 64:DM + (h + 1) * 64, b * S:(b + 1) * S])))(),
                    reads=("qkv0_fm:all",), writes=("a0:k",), dma=True)
            P.step("dve", (lambda h=h: (lambda v: v.tensor_copy(
                vb65[:].rearrange("p (kt f) -> p kt f", f=65)[:, :, 0:64],
                vt_b[:].rearrange("p (kt f) -> p kt f", f=DM)[:, :, h * 64:(h + 1) * 64])))(),
                reads=("a0:vt", "a0:vb65c"), writes=("a0:vb65",))
            for qc in range(4):
                q0 = qc * 512
                for ktp in range(9):
                    if ktp < 8:
                        ktA, ktB = 2 * ktp, 2 * ktp + 1
                        rows = ((0, ktA), (32, ktA), (64, ktB), (96, ktB))

                        def fnqk(t, rows=rows, q0=q0):
                            last = None
                            for i, (r, kt) in enumerate(rows):
                                last = t.matmul(psQK[:, i * 512:(i + 1) * 512],
                                                k_sb[r:r + 32, kt * 128:(kt + 1) * 128],
                                                q_sb[r:r + 32, q0:q0 + 512],
                                                start=True, stop=True,
                                                tile_position=(r, 0))
                            return last
                        P.step("pe", fnqk, reads=("a0:k", "a0:q"),
                               writes=("ps:a0QK",), pe_fence=True)
                        P.step("act", (lambda ktp=ktp: (lambda a: [
                            a.activation(Eb[:, (ktp * 4 + i) * 512:(ktp * 4 + i + 1) * 512],
                                         psQK[:, i * 512:(i + 1) * 512], AF.Exp,
                                         scale=SCALE0) for i in range(4)][-1]))(),
                            reads=("ps:a0QK",), writes=(f"a0:Eg{ktp}",))
                    if ktp >= 1:
                        kp = ktp - 1
                        kA, kB = 2 * kp, 2 * kp + 1

                        def fnpv(t, kA=kA, kB=kB, kp=kp):
                            s0, s1, s2, s3 = (4 * kp + j for j in range(4))
                            st, sp = (kp == 0), (kp == 7)
                            t.matmul(psV[0:65, 0:512], vb65[:, kA * 65:(kA + 1) * 65],
                                     Eb[:, s0 * 512:(s0 + 1) * 512], start=st, stop=False)
                            t.matmul(psV[0:65, 0:512], vb65[:, kB * 65:(kB + 1) * 65],
                                     Eb[:, s2 * 512:(s2 + 1) * 512], start=False, stop=sp)
                            t.matmul(psV[0:65, 512:1024], vb65[:, kA * 65:(kA + 1) * 65],
                                     Eb[:, s1 * 512:(s1 + 1) * 512], start=st, stop=False)
                            return t.matmul(psV[0:65, 512:1024], vb65[:, kB * 65:(kB + 1) * 65],
                                            Eb[:, s3 * 512:(s3 + 1) * 512], start=False, stop=sp)
                        P.step("pe", fnpv,
                               reads=("a0:vb65", f"a0:Eg{kp}"),
                               writes=("ps:a0V",), pe_fence=True)
                P.step("act", (lambda: (lambda a: a.activation(o1[:], psV[0:65, 0:512], AF.Copy)))(),
                       reads=("ps:a0V",), writes=("a0:o1",))
                P.step("act", (lambda: (lambda a: a.activation(o2[:], psV[0:65, 512:1024], AF.Copy)))(),
                       reads=("ps:a0V",), writes=("a0:o2",))
                for src, key, off, dstT, dk in ((o1, "a0:o1", 0, o1T, "a0:o1T"),
                                                (o2, "a0:o2", 512, o2T, "a0:o2T")):
                    def fntr(t, src=src, off=off):
                        last = None
                        for cc in range(4):
                            last = t.matmul(psT[:, off + cc * 65:off + (cc + 1) * 65],
                                            src[0:65, cc * 128:(cc + 1) * 128],
                                            idtf[0:65, 0:65], start=True, stop=True)
                        return last
                    P.step("pe", fntr, reads=(key, "a0:idtf"), writes=("ps:a0T",))
                    P.step("act", (lambda off=off, dstT=dstT: (lambda a: a.activation(
                        dstT[:], psT[:, off:off + 260], AF.Copy)))(),
                        reads=("ps:a0T",), writes=(dk,))
                o1T_v = o1T[:].rearrange("p (c f) -> p c f", f=65)
                o2T_v = o2T[:].rearrange("p (c f) -> p c f", f=65)
                P.step("dve", (lambda: (lambda v: v.reciprocal(rs1T[:], o1T_v[:, :, 64:65].squeeze(2))))(),
                       reads=("a0:o1T",), writes=("a0:rs1",))
                P.step("dve", (lambda: (lambda v: v.reciprocal(rs2T[:], o2T_v[:, :, 64:65].squeeze(2))))(),
                       reads=("a0:o2T",), writes=("a0:rs2",))
                t1T_v = t1T[:].rearrange("p (c f) -> p c f", f=64)
                osbT_v = osbT[:].rearrange("p (c f) -> p c f", f=64)
                P.step("dve", (lambda: (lambda v: v.tensor_tensor(
                    t1T_v, o1T_v[:, :, 0:64],
                    rs1T[:].unsqueeze(2).broadcast_to([128, 4, 64]), ALU.mult)))(),
                    reads=("a0:o1T", "a0:rs1"), writes=("a0:t1",))
                P.step("dve", (lambda: (lambda v: v.tensor_tensor(
                    osbT_v, o2T_v[:, :, 0:64],
                    rs2T[:].unsqueeze(2).broadcast_to([128, 4, 64]), ALU.mult)))(),
                    reads=("a0:o2T", "a0:rs2"), writes=("a0:osb",))
                P.step("dve", (lambda: (lambda v: v.scalar_tensor_tensor(
                    osbT[:], osbT[:], nlam128[:], t1T[:], ALU.mult, ALU.add)))(),
                    reads=("a0:osb", "a0:t1", "a0:nlamb"), writes=("a0:osb",))
                P.step("dve", (lambda: (lambda v: v.tensor_mul(sqT[:], osbT[:], osbT[:])))(),
                       reads=("a0:osb",), writes=("a0:sq",))
                P.step("dve", (lambda: (lambda v: v.reduce_sum(
                    msT[:], sqT[:].rearrange("p (c f) -> p c f", f=64), axis=AX.X)))(),
                    reads=("a0:sq",), writes=("a0:ms",))
                P.step("dve", (lambda: (lambda v: v.tensor_scalar(
                    msT[:], msT[:], 1.0 / 64.0, 1e-6, ALU.mult, ALU.add)))(),
                    reads=("a0:ms",), writes=("a0:ms",))
                P.step("act", (lambda: (lambda a: a.activation(msT[:], msT[:], AF.Ln)))(),
                       reads=("a0:ms",), writes=("a0:ms",))
                P.step("act", (lambda: (lambda a: a.activation(rstdT[:], msT[:], AF.Exp,
                                                               scale=-0.5)))(),
                       reads=("a0:ms",), writes=("a0:rstd",))
                onT_v = onT[:].rearrange("p (c f) -> p c f", f=64)
                P.step("dve", (lambda h=h: (lambda v: v.tensor_tensor(
                    onT_v, osbT_v,
                    nwb[:, h * 64:(h + 1) * 64].unsqueeze(1).broadcast_to([128, 4, 64]),
                    ALU.mult)))(),
                    reads=("a0:osb", "a0:nwb"), writes=("a0:on",))
                P.step("dve", (lambda: (lambda v: v.tensor_tensor(
                    onT_v, onT_v,
                    rstdT[:].unsqueeze(2).broadcast_to([128, 4, 64]), ALU.mult)))(),
                    reads=("a0:on", "a0:rstd"), writes=("a0:on",))
                for cc in range(4):
                    cq = qc * 4 + cc
                    c = cq % 8
                    r0 = 2 * h + cq // 8
                    P.step("dve", (lambda cc=cc, c=c, r0=r0: (lambda v: v.tensor_copy(
                        oft[:, c * S:(c + 1) * S].rearrange("p (d r) -> p d r", r=32)[:, :, r0:r0 + 1],
                        onT[:, cc * 64:(cc + 1) * 64].unsqueeze(2))))(),
                        reads=("a0:on",), writes=(f"a0:oft",))
        for st in range(S // 128):
            pi = st % 2
            pp = psQK[:, pi * 1024:(pi + 1) * 1024]
            pairs = [(oft[:, c * S + st * 128:c * S + (st + 1) * 128],
                      wo_sb[:, c * DM:(c + 1) * DM]) for c in range(8)]
            mm_group(P, pp, pairs,
                     "ps:a0QK", ("a0:oft", "a0:wo"))
            P.step("act", (lambda pi=pi, pp=pp: (lambda a: a.activation(
                evt[:], pp, AF.Copy)))(),
                reads=("ps:a0QK",), writes=("a0:evt",))
            P.step("gp", (lambda b=b, st=st: (lambda g: g.dma_start(
                out=out_ap[b * S + st * 128:b * S + (st + 1) * 128, :], in_=evt[:])))(),
                reads=("a0:evt",), writes=("attn0:all",), dma=True)


def run_standalone(emit_fn, ins_spec, outs_spec, inputs, n_cores=1, sim=False,
                   interm_spec=()):
    nc = bass.Bass("TRN2", target_bir_lowering=False)
    io = {}
    for name, shape, dt in ins_spec:
        io[name] = nc.dram_tensor(name, list(shape), dt, kind="ExternalInput").ap()
    for name, shape, dt in outs_spec:
        io[name] = nc.dram_tensor(name, list(shape), dt, kind="ExternalOutput").ap()
    for name, shape, dt in interm_spec:
        io[name] = nc.dram_tensor(name, list(shape), dt, kind="Internal").ap()
    if "xn_tm" in io:
        io["xnloc_tm"] = io["xn_tm"][0:TL, :]
    with ExitStack() as stack:
        P = Prog(nc, stack)
        emit_fn(P, io, stack)
        P.barrier()
        block = stack.enter_context(nc.Block())
        P.finalize(block)
    if sim:
        from concourse.bass_interp import CoreSim
        s = CoreSim(nc)
        for name, _, _ in ins_spec:
            s.tensor(name)[:] = inputs[name]
        s.simulate(check_with_hw=False)
        return {name: np.array(s.tensor(name)) for name, _, _ in outs_spec}
    res = run_bass_kernel_spmd(nc, [inputs] * n_cores, core_ids=list(range(n_cores)))
    return res.results[0]


def ph_experts(P, io, stack):
    xt_ap = io["Xt_bf"]
    ew_ap = io["ew_bf"]
    eb_ap = io["L0_ffn_expert_b"]
    yt_ap = io["Yt_bf"]
    XtT = sbuf(P, "ex_XtT", [128, 8 * NE], BF16)
    for dc in range(8):
        P.step("sync", (lambda dc=dc: (lambda e: e.dma_start(
            out=XtT[:, dc * NE:(dc + 1) * NE],
            in_=xt_ap[0:NE, dc * 128:(dc + 1) * 128], transpose=True)))(),
            reads=("Xt_bf:all",), writes=("ex:XtT",), dma=True)
    eb_row = sbuf(P, "ex_eb", [1, 16 * DM], BF16)
    ones32 = sbuf(P, "ex_ones", [1, 32], BF16)
    P.step("dve", (lambda: (lambda v: v.memset(ones32[:], 1.0)))(), writes=("ex:ones",))
    wbuf = [sbuf(P, f"ex_w{i}", [128, 8 * DM], BF16) for i in range(2)]
    pE = [psum(P, "ex_psA", [128, 1024]), psum(P, "ex_psB", [128, 1024])]
    yt_sb = sbuf(P, "ex_yt", [32, 2 * DM], BF16)
    for n in range(NEXP):
        wi = n % 2
        if n % 16 == 0:
            P.step("gp", (lambda n=n: (lambda g: g.dma_start(
                out=eb_row[:],
                in_=eb_ap[n:n + 16, :].rearrange("a b -> (a b)")[None, :])))(),
                writes=("ex:eb",), dma=True)
        P.step("sync", (lambda n=n, w=wbuf[wi]: (lambda g: g.dma_start(
            out=w[:].rearrange("p (c f) -> p c f", c=8),
            in_=ew_ap[n].rearrange("(c p) f -> p c f", p=128))))(),
            reads=("ew_bf:all",), writes=(f"ex:w{wi}",), dma=True)

        def fn(t, n=n, wi=wi):
            nn = n % 16
            last = t.matmul(pE[wi][0:32, 0:512], ones32[:],
                            eb_row[:, nn * DM:nn * DM + 512], start=True, stop=False)
            last = t.matmul(pE[wi][0:32, 512:1024], ones32[:],
                            eb_row[:, nn * DM + 512:(nn + 1) * DM], start=True, stop=False)
            for dc in range(8):
                for j in range(2):
                    last = t.matmul(
                        pE[wi][0:32, j * 512:(j + 1) * 512],
                        XtT[:, dc * NE + n * 32:dc * NE + (n + 1) * 32],
                        wbuf[wi][:, dc * DM + j * 512:dc * DM + (j + 1) * 512],
                        start=False, stop=(dc == 7))
            return last
        P.step("pe", fn, reads=(f"ex:w{wi}", "ex:XtT", "ex:eb", "ex:ones"),
               writes=(f"ps:ex{wi}",))
        P.step("act", (lambda wi=wi: (lambda a: a.activation(
            yt_sb[:, wi * DM:(wi + 1) * DM], pE[wi][0:32, :], AF.Copy)))(),
            reads=(f"ps:ex{wi}",), writes=(f"ex:yt{wi}",))
        P.step("gp", (lambda n=n, wi=wi: (lambda g: g.dma_start(
            out=yt_ap[:].rearrange("(q n) d -> q n d", n=64)[:, n, :],
            in_=yt_sb[:, wi * DM:(wi + 1) * DM])))(),
            reads=(f"ex:yt{wi}",), writes=("Yt_bf:all",), dma=True)


def ph_ctrans(P, io, stack):
    ctm_ap = io["C_tm"]
    cp_ap = io["Cp"]
    idt = sbuf(P, "ct_idt", [128, 128], BF16)
    P.step("sync", (lambda: (lambda e: e.dma_start(out=idt[:], in_=io["IDT"][:])))(),
           writes=("ct:idt",), dma=True)
    cin = sbuf(P, "ct_in", [128, NE], BF16)
    grp = sbuf(P, "ct_grp", [128, 16 * 1024], BF16)
    psA = psum(P, "ct_psA", [128, 1024])
    psB = psum(P, "ct_psB", [128, 1024])
    ntile = TL // 128
    for g in range(ntile // 8):
        for ti in range(8):
            t = g * 8 + ti
            P.step("sync", (lambda t=t: (lambda e: e.dma_start(
                out=cin[:], in_=ctm_ap[t * 128:(t + 1) * 128, :])))(),
                reads=("C_tm:all",), writes=("ct:in",), dma=True)
            for half, pp in ((0, psA), (1, psB)):
                def fn(tr, half=half, pp=pp):
                    last = None
                    for j in range(8):
                        ec = half * 8 + j
                        last = tr.matmul(pp[:, j * 128:(j + 1) * 128],
                                         cin[:, ec * 128:(ec + 1) * 128], idt[:],
                                         start=True, stop=True)
                    return last
                P.step("pe", fn, reads=("ct:in", "ct:idt"), writes=(f"ps:ct{half}",))
                P.step("dve", (lambda ti=ti, half=half, pp=pp: (lambda v: v.tensor_copy(
                    grp[:].rearrange("p (ec a f) -> p ec a f", ec=16, a=8)[:, half * 8:(half + 1) * 8, ti, :],
                    pp[:].rearrange("p (j f) -> p j f", j=8))))(),
                    reads=(f"ps:ct{half}",), writes=("ct:grp",))
        for ec in range(16):
            P.step("gp", (lambda g=g, ec=ec: (lambda gp: gp.dma_start(
                out=cp_ap[ec, :, g * 1024:(g + 1) * 1024],
                in_=grp[:].rearrange("p (ec f) -> p ec f", ec=16)[:, ec, :])))(),
                reads=("ct:grp",), writes=("Cp:all",), dma=True)


def ph_y(P, io, stack):
    cp_ap = io["Cp"]
    yt_ap = io["Yt_bf"]
    x_ap = io["x_loc"]
    a0_ap = io["attn0"]
    x1_ap = io["x1"]
    xn1_ap = io["xn1_tm"]
    ys = sbuf(P, "y_ys", [128, 16 * DM], BF16)
    for ec in range(16):
        P.step("sync", (lambda ec=ec: (lambda e: e.dma_start(
            out=ys[:, ec * DM:(ec + 1) * DM], in_=yt_ap[ec * 128:(ec + 1) * 128, :])))(),
            reads=("Yt_bf:all",), writes=("y:ys",), dma=True)
    cps = sbuf(P, "y_cp", [128, 2 * 512], BF16)
    psY = [psum(P, f"y_ps{i}", [128, DM]) for i in range(4)]
    xb = sbuf(P, "y_xb", [128, 4 * DM], F32)
    ab = sbuf(P, "y_ab", [128, 4 * DM], F32)
    x1b = sbuf(P, "y_x1", [128, 4 * DM], F32)
    xn1b = sbuf(P, "y_xn1", [128, 4 * DM], BF16)
    lnb = make_ln_bufs(P, "y_ln", 4)
    for g in range(TL // 512):
        P.step("sync", (lambda g=g: (lambda e: e.dma_start(
            out=xb[:].rearrange("p (a f) -> p a f", a=4),
            in_=x_ap[g * 512:(g + 1) * 512, :].rearrange("(a p) f -> p a f", p=128))))(),
            writes=tuple(f"y:x{i}" for i in range(4)), dma=True)
        P.step("sync", (lambda g=g: (lambda e: e.dma_start(
            out=ab[:].rearrange("p (a f) -> p a f", a=4),
            in_=a0_ap[g * 512:(g + 1) * 512, :].rearrange("(a p) f -> p a f", p=128))))(),
            reads=("attn0:all",), writes=tuple(f"y:a{i}" for i in range(4)), dma=True)
        for ec in range(16):
            ci = ec % 2
            P.step("sync", (lambda g=g, ec=ec, ci=ci: (lambda e: e.dma_start(
                out=cps[:, ci * 512:(ci + 1) * 512],
                in_=cp_ap[ec, :, g * 512:(g + 1) * 512])))(),
                reads=("Cp:all",), writes=(f"y:cp{ci}",), dma=True)
            for ti in range(4):
                mm_group(P, psY[ti][:],
                         [(cps[:, ci * 512 + ti * 128:ci * 512 + (ti + 1) * 128],
                           ys[:, ec * DM:(ec + 1) * DM])],
                         f"ps:y{ti}", (f"y:cp{ci}", "y:ys"),
                         start=(ec == 0), stop=(ec == 15))
        for ti in range(4):
            P.step("dve", (lambda ti=ti: (lambda v: v.tensor_add(
                x1b[:, ti * DM:(ti + 1) * DM], psY[ti][:], xb[:, ti * DM:(ti + 1) * DM])))(),
                reads=(f"ps:y{ti}", f"y:x{ti}"), writes=(f"y:x1{ti}",))
            P.step("dve", (lambda ti=ti: (lambda v: v.tensor_add(
                x1b[:, ti * DM:(ti + 1) * DM], x1b[:, ti * DM:(ti + 1) * DM],
                ab[:, ti * DM:(ti + 1) * DM])))(),
                reads=(f"y:x1{ti}", f"y:a{ti}"), writes=(f"y:x1{ti}",))
        P.step("gp", (lambda g=g: (lambda gp: gp.dma_start(
            out=x1_ap[g * 512:(g + 1) * 512, :].rearrange("(a p) f -> p a f", p=128),
            in_=x1b[:].rearrange("p (a f) -> p a f", a=4))))(),
            reads=tuple(f"y:x1{i}" for i in range(4)), writes=("x1:all",), dma=True)
        ln_tiles(P, lnb, [x1b[:, i * DM:(i + 1) * DM] for i in range(4)],
                 [f"y:x1{i}" for i in range(4)],
                 [xn1b[:, i * DM:(i + 1) * DM] for i in range(4)],
                 [f"y:xn1{i}" for i in range(4)])
        P.step("gp", (lambda g=g: (lambda gp: gp.dma_start(
            out=xn1_ap[g * 512:(g + 1) * 512, :].rearrange("(a p) f -> p a f", p=128),
            in_=xn1b[:].rearrange("p (a f) -> p a f", a=4))))(),
            reads=tuple(f"y:xn1{i}" for i in range(4)), writes=("xn1_tm:all",), dma=True)


def ph_qkv1(P, io, stack):
    w_ap = io["L1_attn_w_qkv"]
    xn_ap = io["xn1_tm"]
    out_ap = io["qkv6_fm"]
    NF = 6144
    fm = sbuf(P, "q1_fm", [128, 8 * TL], BF16)
    for dc in range(8):
        P.step("sync", (lambda dc=dc: (lambda e: e.dma_start(
            out=fm[:, dc * TL:(dc + 1) * TL],
            in_=xn_ap[0:TL, dc * 128:(dc + 1) * 128], transpose=True)))(),
            reads=("xn1_tm:all",), writes=("q1:fm",), dma=True)
    GW = 1024
    wg = [sbuf(P, f"q1_w{i}", [128, 8 * GW], BF16) for i in range(2)]
    ps = [psum(P, "q1_psA", [128, 1024]), psum(P, "q1_psB", [128, 1024])]
    ev = sbuf(P, "q1_ev", [128, 2 * 1024], BF16)
    i = 0
    for fg in range(NF // GW):
        wi = fg % 2
        for dc in range(8):
            P.step("gp", (lambda fg=fg, dc=dc, w=wg[wi]: (lambda g: g.dma_start(
                out=w[:, dc * GW:(dc + 1) * GW],
                in_=w_ap[dc * 128:(dc + 1) * 128, fg * GW:(fg + 1) * GW])))(),
                writes=(f"q1:w{wi}",), dma=True)
        for fc in range(GW // 128):
            for tq in range(TL // 1024):
                pi = i % 2
                i += 1
                pairs = [(wg[wi][:, dc * GW + fc * 128:dc * GW + (fc + 1) * 128],
                          fm[:, dc * TL + tq * 1024:dc * TL + (tq + 1) * 1024])
                         for dc in range(8)]
                mm_group(P, ps[pi][:], pairs, f"ps:q1{pi}", (f"q1:w{wi}", "q1:fm"))
                P.step("act", (lambda pi=pi: (lambda a: a.activation(
                    ev[:, pi * 1024:(pi + 1) * 1024], ps[pi][:], AF.Copy)))(),
                    reads=(f"ps:q1{pi}",), writes=(f"q1:ev{pi}",))
                P.step("gp", (lambda fg=fg, fc=fc, tq=tq, pi=pi: (lambda g: g.dma_start(
                    out=out_ap[fg * GW + fc * 128:fg * GW + (fc + 1) * 128,
                               tq * 1024:(tq + 1) * 1024],
                    in_=ev[:, pi * 1024:(pi + 1) * 1024])))(),
                    reads=(f"q1:ev{pi}",), writes=("qkv6_fm:all",), dma=True)


def ph_attn1(P, io, stack):
    qk_ap = io["qkv6_fm"]
    wout_ap = io["L1_attn_w_out"]
    out_ap = io["attn1"]
    NKT = S // 128
    QH = S // 2
    OQU, OKU, OVU, OQC, OKC, OVC = 0, 1024, 2048, 3072, 4096, 5120
    msl = sbuf(P, "a1_msl", [128, 128], F32)
    mle = sbuf(P, "a1_mle", [128, 128], BF16)
    P.step("sync", (lambda: (lambda e: e.dma_start(out=msl[:], in_=io["MSL"][:])))(),
           writes=("a1:msl",), dma=True)
    P.step("sync", (lambda: (lambda e: e.dma_start(out=mle[:], in_=io["MLE"][:])))(),
           writes=("a1:mle",), dma=True)
    ones_f = sbuf(P, "a1_onesf", [1, 128], F32)
    ones_c = sbuf(P, "a1_onesc", [128, 1], BF16)
    P.step("dve", (lambda: (lambda v: v.memset(ones_f[:], 1.0)))(), writes=("a1:onesf",))
    P.step("dve", (lambda: (lambda v: v.memset(ones_c[:], 1.0)))(), writes=("a1:onesc",))
    wout_sb = sbuf(P, "a1_wout", [128, 8 * DM], BF16)
    for c in range(8):
        P.step("gp", (lambda c=c: (lambda g: g.dma_start(
            out=wout_sb[:, c * DM:(c + 1) * DM], in_=wout_ap[c * 128:(c + 1) * 128, :])))(),
            writes=("a1:wout",), dma=True)
    vu_b = sbuf(P, "a1_vu", [128, NKT * DM], BF16)
    vc_b = sbuf(P, "a1_vc", [128, NKT * DM], BF16)
    qc_s = sbuf(P, "a1_qc", [128, S], BF16)
    kc_s = sbuf(P, "a1_kc", [128, S], BF16)
    qu_s = sbuf(P, "a1_qu", [128, S], BF16)
    ku_s = sbuf(P, "a1_ku", [128, S], BF16)
    sg = sbuf(P, "a1_sg", [128, 128], F32)
    la = sbuf(P, "a1_la", [128, 128], BF16)
    sg2 = sbuf(P, "a1_sg2", [128, 128], F32)
    corr = sbuf(P, "a1_corr", [128, NKT * 128], F32)
    Es = sbuf(P, "a1_E", [128, QH], BF16)
    rs_s = sbuf(P, "a1_rs", [1, QH], F32)
    rb_s = sbuf(P, "a1_rb", [128, QH], F32)
    o_all = sbuf(P, "a1_oall", [128, 8 * S], BF16)
    evt = sbuf(P, "a1_evt", [128, DM], F32)
    psA = psum(P, "a1_psA", [128, 1024])
    psO = psum(P, "a1_psO", [128, 1024])
    psS = psum(P, "a1_psS", [128, 2 * 512])
    for b in range(BL):
        for kt in range(NKT):
            for nm, off, buf, key in (("vu", OVU, vu_b, "a1:vu"), ("vc", OVC, vc_b, "a1:vc")):
                P.step("sync", (lambda off=off, b=b, kt=kt, buf=buf: (lambda e: e.dma_start(
                    out=buf[:, kt * DM:(kt + 1) * DM],
                    in_=qk_ap[off:off + DM, b * S + kt * 128:b * S + (kt + 1) * 128],
                    transpose=True)))(),
                    reads=("qkv6_fm:all",), writes=(key,), dma=True)
        for h in range(H1):
            for nm, off, buf, key in (("qc", OQC, qc_s, "a1:qc"), ("kc", OKC, kc_s, "a1:kc"),
                                      ("qu", OQU, qu_s, "a1:qu"), ("ku", OKU, ku_s, "a1:ku")):
                P.step("sync", (lambda off=off, b=b, h=h, buf=buf: (lambda e: e.dma_start(
                    out=buf[:], in_=qk_ap[off + h * 128:off + (h + 1) * 128,
                                          b * S:(b + 1) * S])))(),
                    reads=("qkv6_fm:all",), writes=(key,), dma=True)
            P.step("dve", (lambda: (lambda v: v.tensor_scalar(qc_s[:], qc_s[:], SCALE1, None, ALU.mult)))(),
                   reads=("a1:qc",), writes=("a1:qc",))
            P.step("dve", (lambda: (lambda v: v.tensor_scalar(qu_s[:], qu_s[:], SCALE1, None, ALU.mult)))(),
                   reads=("a1:qu",), writes=("a1:qu",))
            for qt in range(NKT):
                P.step("pe", (lambda qt=qt: (lambda t: t.matmul(
                    psA[:, 0:128], ku_s[:, qt * 128:(qt + 1) * 128],
                    qu_s[:, qt * 128:(qt + 1) * 128], start=True, stop=True)))(),
                    reads=("a1:ku", "a1:qu"), writes=("ps:a1A",))
                P.step("act", (lambda: (lambda a: a.activation(sg[:], psA[:, 0:128], AF.Sigmoid)))(),
                       reads=("ps:a1A",), writes=("a1:sg",))
                P.step("dve", (lambda: (lambda v: v.tensor_mul(la[:], sg[:], msl[:])))(),
                       reads=("a1:sg", "a1:msl"), writes=("a1:la",))
                P.step("pe", (lambda qt=qt, h=h: (lambda t: t.matmul(
                    psO[:, 0:128], vu_b[:, qt * DM + h * 128:qt * DM + (h + 1) * 128],
                    la[:], start=True, stop=True)))(),
                    reads=("a1:vu", "a1:la"), writes=("ps:a1O",))
                P.step("act", (lambda: (lambda a: a.activation(sg2[:], psO[:, 0:128], AF.Sigmoid)))(),
                       reads=("ps:a1O",), writes=("a1:sg2",))
                P.step("dve", (lambda qt=qt: (lambda v: v.tensor_mul(
                    corr[:, qt * 128:(qt + 1) * 128], psO[:, 0:128], sg2[:])))(),
                    reads=("ps:a1O", "a1:sg2"), writes=("a1:corr",))
            for qh in range(2):
                nkt_h = (qh + 1) * (QH // 128)
                cs_max = max(0, (nkt_h - 1) * 128 - qh * QH)
                if cs_max > 0:
                    P.step("dve", (lambda cs_max=cs_max: (lambda v: v.memset(
                        Es[:, 0:cs_max], 0.0)))(), writes=("a1:E",))
                for kt in range(nkt_h - 1, -1, -1):
                    cs = max(0, kt * 128 - qh * QH)
                    qsl = slice(qh * QH + cs, (qh + 1) * QH)
                    diag = (qh * QH <= kt * 128 < (qh + 1) * QH)
                    P.step("pe", (lambda kt=kt, cs=cs, qh=qh: (lambda t:
                        _smm(t, psA, kc_s, qc_s, kt, cs, qh, QH)))(),
                        reads=("a1:kc", "a1:qc"), writes=("ps:a1A",))
                    if diag:
                        dc0 = kt * 128 - qh * QH
                        P.step("dve", (lambda kt=kt, dc0=dc0: (lambda v: v.tensor_sub(
                            psA[:, dc0:dc0 + 128], psA[:, dc0:dc0 + 128],
                            corr[:, kt * 128:(kt + 1) * 128])))(),
                            reads=("ps:a1A", "a1:corr"), writes=("ps:a1A",))
                    P.step("act", (lambda cs=cs: (lambda a: a.activation(
                        Es[:, cs:QH], psA[:, cs:QH], AF.Exp)))(),
                        reads=("ps:a1A",), writes=("a1:E",))
                    if diag:
                        dc0 = kt * 128 - qh * QH
                        P.step("dve", (lambda dc0=dc0: (lambda v: v.tensor_mul(
                            Es[:, dc0:dc0 + 128], Es[:, dc0:dc0 + 128], mle[:])))(),
                            reads=("a1:E", "a1:mle"), writes=("a1:E",))
                    P.step("pe", (lambda kt=kt, h=h, st=(kt == nkt_h - 1), sp=(kt == 0): (lambda t:
                        _pvmm(t, psO, vc_b, Es, kt, h, QH, st, sp)))(),
                        reads=("a1:vc", "a1:E"), writes=("ps:a1O",))
                    P.step("pe", (lambda st=(kt == nkt_h - 1), sp=(kt == 0): (lambda t:
                        _summ(t, psS, ones_c, Es, QH, st, sp)))(),
                        reads=("a1:onesc", "a1:E"), writes=("ps:a1S",))
                P.step("dve", (lambda: (lambda v: v.reciprocal(rs_s[:, 0:QH], psS[0:1, 0:QH])))(),
                       reads=("ps:a1S",), writes=("a1:rs",))
                P.step("pe", (lambda: (lambda t: [
                    t.matmul(psS[0:128, c0:c0 + 512] if False else psS[:, c0:c0 + 512],
                             ones_f[:], rs_s[:, c0:c0 + 512], start=True, stop=True)
                    for c0 in range(0, QH, 512)][-1]))(),
                    reads=("a1:rs", "a1:onesf"), writes=("ps:a1S",))
                P.step("act", (lambda: (lambda a: a.activation(rb_s[:, 0:QH], psS[:, 0:QH], AF.Copy)))(),
                       reads=("ps:a1S",), writes=("a1:rb",))
                P.step("dve", (lambda h=h, qh=qh: (lambda v: v.tensor_mul(
                    o_all[:, h * S + qh * QH:h * S + (qh + 1) * QH], psO[:, 0:QH], rb_s[:, 0:QH])))(),
                    reads=("ps:a1O", "a1:rb"), writes=("a1:oall",))
        for st in range(S // 128):
            pairs = [(o_all[:, hh * S + st * 128:hh * S + (st + 1) * 128],
                      wout_sb[:, hh * DM:(hh + 1) * DM]) for hh in range(H1)]
            mm_group(P, psA[:, 0:DM], pairs, "ps:a1A", ("a1:oall", "a1:wout"))
            P.step("act", (lambda: (lambda a: a.activation(evt[:], psA[:, 0:DM], AF.Copy)))(),
                   reads=("ps:a1A",), writes=("a1:evt",))
            P.step("gp", (lambda b=b, st=st: (lambda g: g.dma_start(
                out=out_ap[b * S + st * 128:b * S + (st + 1) * 128, :], in_=evt[:])))(),
                reads=("a1:evt",), writes=("attn1:all",), dma=True)


def col_slices(cs, hi):
    out = []
    c0 = cs
    if c0 % 512:
        c1 = min((c0 // 512 + 1) * 512, hi)
        out.append((c0, c1))
        c0 = c1
    while c0 < hi:
        c1 = min(c0 + 512, hi)
        out.append((c0, c1))
        c0 = c1
    return out


def _smm(t, psA, kc_s, qc_s, kt, cs, qh, QH):
    last = None
    for c0, c1 in col_slices(cs, QH):
        last = t.matmul(psA[:, c0:c1], kc_s[:, kt * 128:(kt + 1) * 128],
                        qc_s[:, qh * QH + c0:qh * QH + c1], start=True, stop=True)
    return last


def _pvmm(t, psO, vc_b, Es, kt, h, QH, start, stop):
    last = None
    for c0 in range(0, QH, 512):
        last = t.matmul(psO[:, c0:c0 + 512], vc_b[:, kt * DM + h * 128:kt * DM + (h + 1) * 128],
                        Es[:, c0:c0 + 512], start=start, stop=stop)
    return last


def _summ(t, psS, ones_c, Es, QH, start, stop):
    last = None
    for c0 in range(0, QH, 512):
        last = t.matmul(psS[0:1, c0:c0 + 512], ones_c[:], Es[:, c0:c0 + 512], start=start, stop=stop)
    return last


def ph_ffn1(P, io, stack):
    xn_ap = io["xn1_tm"]
    x1_ap = io["x1"]
    a1_ap = io["attn1"]
    out_ap = io["OUT"]
    w1p_ap, w2p_ap = io["w1p"], io["w2p"]
    h1p_ap, h2p_ap = io["h1p"], io["h2p"]
    w3p_ap = io["w3p"]
    idtf = sbuf(P, "f_idtf", [128, 128], F32)
    P.step("sync", (lambda: (lambda e: e.dma_start(out=idtf[:], in_=io["IDTF"][:])))(),
           writes=("f:idtf",), dma=True)
    ones_f = sbuf(P, "f_onesf", [1, 128], F32)
    P.step("dve", (lambda: (lambda v: v.memset(ones_f[:], 1.0)))(), writes=("f:onesf",))
    s_raw = sbuf(P, "f_sraw", [32, 4 * 128], F32)
    ssb = sbuf(P, "f_s", [128, 4 * 32], F32)
    psS = psum(P, "f_psS", [128, 512])
    for i, nm in enumerate(["L1_ffn_w1_s", "L1_ffn_w2_s", "L1_ffn_h1_s", "L1_ffn_h2_s"]):
        P.step("sync", (lambda nm=nm, i=i: (lambda e: e.dma_start(
            out=s_raw[:, i * 128:(i + 1) * 128],
            in_=io[nm][:].rearrange("(c p) -> c p", p=128))))(),
            writes=("f:sraw",), dma=True)
    for i in range(4):
        P.step("pe", (lambda i=i: (lambda t: t.matmul(
            psS[:, i * 32:(i + 1) * 32], s_raw[:, i * 128:(i + 1) * 128],
            idtf[0:32, 0:32], start=True, stop=True)))(),
            reads=("f:sraw", "f:idtf"), writes=("ps:fS",))
    P.step("act", (lambda: (lambda a: a.activation(ssb[:], psS[:, 0:128], AF.Copy)))(),
           reads=("ps:fS",), writes=("f:s",))
    s3row = sbuf(P, "f_s3row", [1, DM], F32)
    w3row = sbuf(P, "f_w3row", [128, DM], F32)
    P.step("sync", (lambda: (lambda e: e.dma_start(out=s3row[:], in_=io["L1_ffn_w3_s"][None, :])))(),
           writes=("f:s3r",), dma=True)
    psW = psum(P, "f_psW", [128, 512])
    for j in range(2):
        P.step("pe", (lambda j=j: (lambda t: t.matmul(
            psW[:], ones_f[:], s3row[:, j * 512:(j + 1) * 512], start=True, stop=True)))(),
            reads=("f:s3r", "f:onesf"), writes=("ps:fW",))
        P.step("act", (lambda j=j: (lambda a: a.activation(
            w3row[:, j * 512:(j + 1) * 512], psW[:], AF.Copy)))(),
            reads=("ps:fW",), writes=("f:w3row",))
    T = 1024
    xfm = sbuf(P, "f_xfm", [128, 8 * T], BF16)
    ub = sbuf(P, "f_u", [128, 32 * T], BF16)
    gb = sbuf(P, "f_g", [128, 32 * T], BF16)
    pan = sbuf(P, "f_pan", [128, 32 * 512], BF16)
    ps2 = [psum(P, "f_psA", [128, T]), psum(P, "f_psB", [128, T])]
    evb = sbuf(P, "f_ev", [128, T], BF16)
    xtb = sbuf(P, "f_xt", [128, DM], F32)
    atb = sbuf(P, "f_at", [128, DM], F32)
    otb = sbuf(P, "f_ot", [128, DM], F32)
    npass = TL // T
    for p_i in range(npass):
        tok0 = p_i * T
        for dc in range(8):
            P.step("sync", (lambda dc=dc, tok0=tok0: (lambda e: e.dma_start(
                out=xfm[:, dc * T:(dc + 1) * T],
                in_=xn_ap[tok0:tok0 + T, dc * 128:(dc + 1) * 128], transpose=True)))(),
                reads=("xn1_tm:all",), writes=("f:xfm",), dma=True)
        for stage in range(2):
            wp_ap, wpn = ((w1p_ap, "w1p"), (w2p_ap, "w2p"))[stage]
            for og in range(8):
                for hf in range(2):
                    P.step("sync", (lambda og=og, hf=hf, wp_ap=wp_ap: (lambda e: e.dma_start(
                        out=pan[:, hf * 4 * 512:(hf + 1) * 4 * 512]
                            .rearrange("p (c o) -> p c o", c=4),
                        in_=wp_ap[hf * 4:(hf + 1) * 4, :, og * 512:(og + 1) * 512]
                            .rearrange("c p o -> p c o"))))(),
                        reads=(wpn + ":all",), writes=(f"f:pan{hf}",), dma=True)
                for oc4 in range(4):
                    oc = og * 4 + oc4
                    pi = oc % 2
                    for hf in range(2):
                        pairs = [(pan[:, ic * 512 + oc4 * 128:ic * 512 + (oc4 + 1) * 128],
                                  xfm[:, ic * T:(ic + 1) * T]) for ic in range(hf * 4, hf * 4 + 4)]
                        mm_group(P, ps2[pi][:], pairs, f"ps:f{pi}",
                                 (f"f:pan{hf}", "f:xfm"),
                                 start=(hf == 0), stop=(hf == 1))
                    if stage == 0:
                        P.step("act", (lambda oc=oc, pi=pi: (lambda a: a.activation(
                            ub[:, oc * T:(oc + 1) * T], ps2[pi][:], AF.Silu,
                            scale=ssb[:, oc:oc + 1])))(),
                            reads=(f"ps:f{pi}", "f:s"), writes=(f"f:ub{oc}",))
                    else:
                        P.step("act", (lambda oc=oc, pi=pi: (lambda a: a.activation(
                            evb[:], ps2[pi][:], AF.Copy,
                            scale=ssb[:, 32 + oc:32 + oc + 1])))(),
                            reads=(f"ps:f{pi}", "f:s"), writes=("f:ev",))
                        P.step("dve", (lambda oc=oc: (lambda v: v.tensor_mul(
                            ub[:, oc * T:(oc + 1) * T], ub[:, oc * T:(oc + 1) * T],
                            evb[:])))(),
                            reads=(f"f:ub{oc}", "f:ev"), writes=(f"f:ub{oc}",))
        for stage in range(2):
            hp_ap, hpn = ((h1p_ap, "h1p"), (h2p_ap, "h2p"))[stage]
            sc0 = 64 + stage * 32
            src, dst = ((ub, gb), (gb, ub))[stage]
            ksrc, kdst = (("f:ub", "f:gb"), ("f:gb", "f:ub"))[stage]
            for og in range(8):
                for hf in range(2):
                    P.step("sync", (lambda og=og, hf=hf, hp_ap=hp_ap: (lambda e: e.dma_start(
                        out=pan[:, hf * 16 * 512:(hf + 1) * 16 * 512]
                            .rearrange("p (c o) -> p c o", c=16),
                        in_=hp_ap[hf * 16:(hf + 1) * 16, :, og * 512:(og + 1) * 512]
                            .rearrange("c p o -> p c o"))))(),
                        reads=(hpn + ":all",), writes=(f"f:pan{hf}",), dma=True)
                for oc4 in range(4):
                    oc = og * 4 + oc4
                    pi = oc % 2
                    for hf in range(2):
                        pairs = [(pan[:, kc * 512 + oc4 * 128:kc * 512 + (oc4 + 1) * 128],
                                  src[:, kc * T:(kc + 1) * T]) for kc in range(hf * 16, hf * 16 + 16)]
                        mm_group(P, ps2[pi][:], pairs, f"ps:f{pi}",
                                 (f"f:pan{hf}",) + tuple(f"{ksrc}{kc}" for kc in range(hf * 16, hf * 16 + 16)),
                                 start=(hf == 0), stop=(hf == 1))
                    P.step("act", (lambda oc=oc, pi=pi, sc0=sc0, dst=dst: (lambda a: a.activation(
                        dst[:, oc * T:(oc + 1) * T], ps2[pi][:], AF.Gelu,
                        scale=ssb[:, sc0 + oc:sc0 + oc + 1])))(),
                        reads=(f"ps:f{pi}", "f:s"), writes=(f"{kdst}{oc}",))
        for kc in range(32):
            P.step("sync", (lambda kc=kc: (lambda e: e.dma_start(
                out=gb[:, kc * T:kc * T + DM],
                in_=w3p_ap[kc * 128:(kc + 1) * 128, :])))(),
                reads=("w3p:all",), writes=(f"f:gb{kc}",), dma=True)
        for ti in range(T // 128):
            pi = ti % 2
            pairs = [(ub[:, kc * T + ti * 128:kc * T + (ti + 1) * 128],
                      gb[:, kc * T:kc * T + DM]) for kc in range(32)]
            mm_group(P, ps2[pi][:, 0:DM], pairs, f"ps:f{pi}",
                     tuple(f"f:ub{kc}" for kc in range(32)) +
                     tuple(f"f:gb{kc}" for kc in range(32)))
            P.step("sync", (lambda ti=ti, tok0=tok0: (lambda e: e.dma_start(
                out=xtb[:], in_=x1_ap[tok0 + ti * 128:tok0 + (ti + 1) * 128, :])))(),
                reads=("x1:all",), writes=("f:xt",), dma=True)
            P.step("sync", (lambda ti=ti, tok0=tok0: (lambda e: e.dma_start(
                out=atb[:], in_=a1_ap[tok0 + ti * 128:tok0 + (ti + 1) * 128, :])))(),
                reads=("attn1:all",), writes=("f:at",), dma=True)
            P.step("dve", (lambda pi=pi: (lambda v: v.tensor_mul(
                otb[:], ps2[pi][:, 0:DM], w3row[:])))(),
                reads=(f"ps:f{pi}", "f:w3row"), writes=("f:ot",))
            P.step("dve", (lambda: (lambda v: v.tensor_add(otb[:], otb[:], xtb[:])))(),
                   reads=("f:ot", "f:xt"), writes=("f:ot",))
            P.step("dve", (lambda: (lambda v: v.tensor_add(otb[:], otb[:], atb[:])))(),
                   reads=("f:ot", "f:at"), writes=("f:ot",))
            P.step("gp", (lambda ti=ti, tok0=tok0: (lambda g: g.dma_start(
                out=out_ap[tok0 + ti * 128:tok0 + (ti + 1) * 128, :], in_=otb[:])))(),
                reads=("f:ot",), writes=("OUT:all",), dma=True)


def ph_cast_stream(P, io, stack):
    s_ap, d_ap = io["L0_ffn_phi_w"], io["phi_bf"]
    for i in range(2):
        P.step("gp", (lambda i=i, s=s_ap, d=d_ap: (lambda g: g.dma_start(
            out=d[i * 1024:(i + 1) * 1024, :], in_=s[i * 1024:(i + 1) * 1024, :])))(),
            writes=(("phi_bf:all",) if i == 1 else ()), dma=True, dsem="cphi")
    mats = [
        ("L1_ffn_w1_w", "w1_bf", 4096, 1024),
        ("L1_ffn_w2_w", "w2_bf", 4096, 1024),
        ("L1_ffn_w3_w", "w3_bf", 1024, 4096),
        ("L1_ffn_h1_w", "h1_bf", 4096, 4096),
        ("L1_ffn_h2_w", "h2_bf", 4096, 4096),
    ]
    chunks = []
    for src, dst, rows, cols in mats:
        rstep = max(128, (1 << 20) // cols)
        for r0 in range(0, rows, rstep):
            chunks.append((src, dst, r0, min(r0 + rstep, rows)))
    for i, (src, dst, r0, r1) in enumerate(chunks):
        last = (i == len(chunks) - 1)
        P.step("gp", (lambda s=io[src], d=io[dst], r0=r0, r1=r1: (lambda g: g.dma_start(
            out=d[r0:r1, :], in_=s[r0:r1, :])))(),
            writes=(tuple(m[1] + ":all" for m in mats) if last else ()),
            dma=True, dsem="cffn")
    ew_s, ew_d = io["L0_ffn_expert_w"], io["ew_bf"]
    for n in range(NEXP):
        P.step("gp", (lambda n=n: (lambda g: g.dma_start(
            out=ew_d[n], in_=ew_s[n])))(),
            writes=(("ew_bf:all",) if n == NEXP - 1 else ()), dma=True, dsem="cew")

def ph_panel_ops(P, io, stack):
    NBG = 6
    tp = [sbuf(P, f"bg_tp{i}", [128, DH], BF16) for i in range(NBG)]
    panel_jobs = []
    specs = [
        ("w1_bf", "w1p", 4096, 1024),
        ("w2_bf", "w2p", 4096, 1024),
        ("h1_bf", "h1p", 4096, 4096),
        ("h2_bf", "h2p", 4096, 4096),
    ]
    for src_bf, dst_p, odim, idim in specs:
        s_ap, d_ap = io[src_bf], io[dst_p]
        nic = idim // 128
        for ic in range(nic):
            panel_jobs.append((
                src_bf, dst_p + ":all" if ic == nic - 1 else None,
                (lambda s=s_ap, ic=ic, odim=odim: (lambda t: (lambda e: e.dma_start(
                    out=t[:, 0:odim], in_=s[0:odim, ic * 128:(ic + 1) * 128],
                    transpose=True))))(),
                (lambda d=d_ap, ic=ic, odim=odim: (lambda t: (lambda e: e.dma_start(
                    out=d[ic], in_=t[:, 0:odim]))))(),
            ))
    s_ap, d_ap = io["w3_bf"], io["w3p"]
    for kc in range(DH // 128):
        panel_jobs.append((
            "w3_bf", "w3p:all" if kc == DH // 128 - 1 else None,
            (lambda s=s_ap, kc=kc: (lambda t: (lambda e: e.dma_start(
                out=t[:, 0:DM], in_=s[0:DM, kc * 128:(kc + 1) * 128],
                transpose=True))))(),
            (lambda d=d_ap, kc=kc: (lambda t: (lambda e: e.dma_start(
                out=d[kc * 128:(kc + 1) * 128, :], in_=t[:, 0:DM]))))(),
        ))

    def mk_tr(i):
        src_bf, _, tr_fn, _ = panel_jobs[i]
        return lambda: P.step("sync", tr_fn(tp[i % NBG]), reads=(src_bf + ":all",),
                              writes=(f"bgtp{i % NBG}",), dma=True)

    def mk_st(i):
        _, wkey, _, st_fn = panel_jobs[i]
        return lambda: P.step("sync", st_fn(tp[i % NBG]), reads=(f"bgtp{i % NBG}",),
                              writes=(wkey,) if wkey else (), dma=True)

    nj = len(panel_jobs)
    ops = []
    for i in range(min(NBG, nj)):
        ops.append(mk_tr(i))
    for i in range(nj):
        ops.append(mk_st(i))
        if i + NBG < nj:
            ops.append(mk_tr(i + NBG))
    return ops


def ph_panel(P, io, stack):
    for op in ph_panel_ops(P, io, stack):
        op()


def make_ln_bufs(P, name, G):
    return {
        "st": sbuf(P, f"lns_{name}", [128, G * 12], F32),
        "st2": sbuf(P, f"lna_{name}", [128, G * 2], F32),
        "rstd": sbuf(P, f"lnr_{name}", [128, G], F32),
        "nb": sbuf(P, f"lnb_{name}", [128, G], F32),
        "name": name, "G": G,
    }


def ln_tiles(P, bufs, src_slices, src_keys, dst_slices, dst_keys, eps=1e-5,
             act_func=AF.Copy, n_free=DM):
    G = len(src_slices)
    assert G <= bufs["G"]
    tag = bufs["name"]
    st, st2, rstd, nb = bufs["st"], bufs["st2"], bufs["rstd"], bufs["nb"]
    ks, ka, kr, kb = f"lns:{tag}", f"lna:{tag}", f"lnr:{tag}", f"lnb:{tag}"
    nhalf = n_free // 2
    for i, sl in enumerate(src_slices):
        P.step("dve", (lambda sl=sl, i=i: (lambda v: v.bn_stats(st[:, i * 12:i * 12 + 6], sl[:, 0:nhalf])))(),
               reads=(src_keys[i],), writes=(ks,))
        P.step("dve", (lambda sl=sl, i=i: (lambda v: v.bn_stats(st[:, i * 12 + 6:i * 12 + 12], sl[:, nhalf:n_free])))(),
               reads=(src_keys[i],), writes=(ks,))
    for i in range(G):
        P.step("dve", (lambda i=i: (lambda v: v.bn_aggr(st2[:, i * 2:i * 2 + 2], st[:, i * 12:i * 12 + 12])))(),
               reads=(ks,), writes=(ka,))
    var_view = st2[:].rearrange("p (g two) -> p g two", two=2)[:, :, 1:2].squeeze(2)
    mean_view = st2[:].rearrange("p (g two) -> p g two", two=2)[:, :, 0:1].squeeze(2)
    P.step("dve", (lambda: (lambda v: v.tensor_scalar(rstd[:], var_view, eps, None, ALU.add)))(),
           reads=(ka,), writes=(kr,))
    P.step("act", (lambda: (lambda a: a.activation(rstd[:], rstd[:], AF.Sqrt)))(),
           reads=(kr,), writes=(kr,))
    P.step("dve", (lambda: (lambda v: v.reciprocal(rstd[:], rstd[:])))(),
           reads=(kr,), writes=(kr,))
    if act_func == AF.Copy:
        P.step("dve", (lambda: (lambda v: v.tensor_scalar(nb[:], mean_view, -1.0, None, ALU.mult)))(),
               reads=(ka,), writes=(kb,))
        for i, (sl, dst) in enumerate(zip(src_slices, dst_slices)):
            P.step("dve", (lambda sl=sl, dst=dst, i=i: (lambda v: v.tensor_scalar(
                dst, sl, nb[:, i:i + 1], rstd[:, i:i + 1], ALU.add, ALU.mult)))(),
                reads=(src_keys[i], kr, kb), writes=(dst_keys[i],))
    else:
        P.step("dve", (lambda: (lambda v: v.tensor_tensor(nb[:], mean_view, rstd[:], ALU.mult)))(),
               reads=(ka, kr), writes=(kb,))
        P.step("dve", (lambda: (lambda v: v.tensor_scalar(nb[:], nb[:], -1.0, None, ALU.mult)))(),
               reads=(kb,), writes=(kb,))
        for i, (sl, dst) in enumerate(zip(src_slices, dst_slices)):
            P.step("act", (lambda sl=sl, dst=dst, i=i: (lambda a: a.activation(
                dst, sl, act_func, bias=nb[:, i:i + 1], scale=rstd[:, i:i + 1])))(),
                reads=(src_keys[i], kr, kb), writes=(dst_keys[i],))


def ph_lnA(P, io, stack, n_tok, x_name, out_name, tag, bgops=None):
    G = 8
    xb = sbuf(P, f"ln{tag}_x", [128, G * DM], F32)
    xn = sbuf(P, f"ln{tag}_o", [128, G * DM], BF16)
    lnb = make_ln_bufs(P, f"ln{tag}", G)
    x_ap, o_ap = io[x_name], io[out_name]
    ngroups = n_tok // (128 * G)
    bgops = list(bgops) if bgops else []
    bg_start = ngroups // 2
    per_g = ceil_div(len(bgops), max(1, ngroups - bg_start)) if bgops else 0
    bi = 0
    for g in range(ngroups):
        if g >= bg_start:
            for _ in range(per_g):
                if bi < len(bgops):
                    bgops[bi]()
                    bi += 1
        kx, ko = f"ln{tag}x", f"ln{tag}o"
        P.step("sync", (lambda g=g: (lambda e: e.dma_start(
            out=xb[:].rearrange("p (a f) -> p a f", a=G),
            in_=x_ap[g * G * 128:(g + 1) * G * 128, :].rearrange("(a p) f -> p a f", p=128))))(),
            writes=tuple(f"{kx}{i}" for i in range(G)), dma=True)
        src = [xb[:, i * DM:(i + 1) * DM] for i in range(G)]
        dst = [xn[:, i * DM:(i + 1) * DM] for i in range(G)]
        ln_tiles(P, lnb, src, [f"{kx}{i}" for i in range(G)],
                 dst, [f"{ko}{i}" for i in range(G)])
        P.step("sync", (lambda g=g: (lambda gp: gp.dma_start(
            out=o_ap[g * G * 128:(g + 1) * G * 128, :].rearrange("(a p) f -> p a f", p=128),
            in_=xn[:].rearrange("p (a f) -> p a f", a=G))))(),
            reads=tuple(f"{ko}{i}" for i in range(G)), writes=(f"{out_name}:all",), dma=True)
    while bi < len(bgops):
        bgops[bi]()
        bi += 1


def ph_logits(P, io, stack, local_lo, local_hi):
    nc = P.nc
    xn_ap = io["xn_tm"]
    phi_ap = io["phi_bf"]
    phib_ap = io["L0_ffn_phi_b"]
    ctm_ap = io["C_tm"]
    xt_ap = io["Xt_bf"]

    phiT = sbuf(P, "lg_phiT", [128, 8 * NE], BF16)
    for kc in range(8):
        P.step("sync", (lambda kc=kc: (lambda e: e.dma_start(
            out=phiT[:, kc * NE:(kc + 1) * NE],
            in_=phi_ap[0:NE, kc * 128:(kc + 1) * 128], transpose=True)))(),
            reads=("phi_bf:all",), writes=("lg:phiT",), dma=True)
    ones_f = sbuf(P, "lg_ones_f", [1, 128], F32)
    phib_row = sbuf(P, "lg_phib_row", [1, NE], F32)
    expb = sbuf(P, "lg_expb", [128, NE], BF16)
    lgA = psum(P, "lg_psA", [128, 1024])
    lgB = psum(P, "lg_psB", [128, 1024])
    xps = [psum(P, "lg_xpsA", [128, DM]), psum(P, "lg_xpsB", [128, DM])]
    P.step("sync", (lambda: (lambda e: e.dma_start(out=phib_row[:], in_=phib_ap[None, :])))(),
           writes=("lg:pbrow",), dma=True)
    P.step("dve", (lambda: (lambda v: v.memset(ones_f[:], 1.0)))(), writes=("lg:ones",))
    for j in range(2):
        ps_j = (lgA, lgB)[j]
        P.step("pe", (lambda j=j, ps_j=ps_j: (lambda t: (
            t.matmul(ps_j[:, 0:512], ones_f[:], phib_row[:, j * 1024:j * 1024 + 512],
                     start=True, stop=True),
            t.matmul(ps_j[:, 512:1024], ones_f[:], phib_row[:, j * 1024 + 512:(j + 1) * 1024],
                     start=True, stop=True))[-1]))(),
            reads=("lg:pbrow", "lg:ones"), writes=(f"ps:lg{j}",))
        P.step("act", (lambda j=j, ps_j=ps_j: (lambda a: a.activation(
            expb[:, j * 1024:(j + 1) * 1024], ps_j[:], AF.Exp)))(),
            reads=(f"ps:lg{j}",), writes=("lg:expb",))

    fm = sbuf(P, "lg_fm", [128, 8 * 1024], BF16)
    tm2 = sbuf(P, "lg_tm", [128, 8 * DM], BF16)
    Dbuf = sbuf(P, "lg_D", [128, 8 * NE], BF16)
    Ebuf2 = sbuf(P, "lg_E", [128, 2 * NE], BF16)
    Cbuf2 = sbuf(P, "lg_C", [128, 2 * NE], BF16)
    sD = sbuf(P, "lg_sD", [128, 2 * 64], F32)
    rD = sbuf(P, "lg_rD", [128, 2 * 64], F32)
    sC = sbuf(P, "lg_sC", [128, 2 * 32], F32)
    rC = sbuf(P, "lg_rC", [128, 2 * 32], F32)
    Xacc = sbuf(P, "lg_Xacc", [128, 16 * DM], F32)

    n_half = TA // 1024
    for hs in range(n_half):
        t0 = hs * 8
        tm = tm2[:, :]
        ktm = "lg:tm"
        P.step("sync", (lambda hs=hs, tm=tm: (lambda e: e.dma_start(
            out=tm.rearrange("p (a f) -> p a f", a=8),
            in_=xn_ap[hs * 1024:(hs + 1) * 1024, :].rearrange("(a p) f -> p a f", p=128))))(),
            reads=("xn_tm:all",), writes=(ktm,), dma=True)
        for kc in range(8):
            P.step("sync", (lambda hs=hs, kc=kc: (lambda e: e.dma_start(
                out=fm[:, kc * 1024:(kc + 1) * 1024],
                in_=xn_ap[hs * 1024:(hs + 1) * 1024, kc * 128:(kc + 1) * 128],
                transpose=True)))(),
                reads=("xn_tm:all",), writes=("lg:fm",), dma=True)
        for ti in range(8):
            t = t0 + ti
            tp_ = ti % 2
            Ebuf = Ebuf2[:, tp_ * NE:(tp_ + 1) * NE]
            Cbuf = Cbuf2[:, tp_ * NE:(tp_ + 1) * NE]
            sDv, rDv = sD[:, tp_ * 64:(tp_ + 1) * 64], rD[:, tp_ * 64:(tp_ + 1) * 64]
            sCv, rCv = sC[:, tp_ * 32:(tp_ + 1) * 32], rC[:, tp_ * 32:(tp_ + 1) * 32]
            kE, kC = f"lg:E{tp_}", f"lg:C{tp_}"
            ksD, krD = f"lg:sD{tp_}", f"lg:rD{tp_}"
            ksC, krC = f"lg:sC{tp_}", f"lg:rC{tp_}"
            for j, ps_j in ((0, lgA), (1, lgB)):
                pairs = [(fm[:, kc * 1024 + ti * 128: kc * 1024 + (ti + 1) * 128],
                          phiT[:, kc * NE + j * 1024: kc * NE + (j + 1) * 1024])
                         for kc in range(8)]
                mm_group(P, ps_j[:], pairs, f"ps:lg{j}", ("lg:fm", "lg:phiT"))
                P.step("act", (lambda j=j, ps_j=ps_j, Ebuf=Ebuf: (lambda a: a.activation(
                    Ebuf[:, j * 1024:(j + 1) * 1024], ps_j[:], AF.Exp)))(),
                    reads=(f"ps:lg{j}",), writes=(kE,))
            P.step("dve", (lambda Ebuf=Ebuf: (lambda v: v.tensor_mul(Ebuf, Ebuf, expb[:])))(),
                   reads=(kE, "lg:expb"), writes=(kE,))
            P.step("dve", (lambda Ebuf=Ebuf, sDv=sDv: (lambda v: v.reduce_sum(
                sDv, Ebuf.rearrange("p (g k) -> p g k", k=32), axis=AX.X)))(),
                reads=(kE,), writes=(ksD,))
            P.step("dve", (lambda sDv=sDv, rDv=rDv: (lambda v: v.reciprocal(rDv, sDv)))(),
                   reads=(ksD,), writes=(krD,))
            P.step("dve", (lambda ti=ti, Ebuf=Ebuf, rDv=rDv: (lambda v: v.tensor_tensor(
                Dbuf[:, ti * NE:(ti + 1) * NE].rearrange("p (g k) -> p g k", k=32),
                Ebuf.rearrange("p (g k) -> p g k", k=32),
                rDv.unsqueeze(2).broadcast_to([128, 64, 32]),
                ALU.mult)))(),
                reads=(kE, krD), writes=("lg:D",))
            if local_lo <= t * 128 < local_hi:
                P.step("dve", (lambda Ebuf=Ebuf, sCv=sCv: (lambda v: v.reduce_sum(
                    sCv, Ebuf.rearrange("p (g k) -> p k g", k=32), axis=AX.X)))(),
                    reads=(kE,), writes=(ksC,))
                P.step("dve", (lambda sCv=sCv, rCv=rCv: (lambda v: v.reciprocal(rCv, sCv)))(),
                       reads=(ksC,), writes=(krC,))
                P.step("dve", (lambda Ebuf=Ebuf, Cbuf=Cbuf, rCv=rCv: (lambda v: v.tensor_tensor(
                    Cbuf.rearrange("p (k g) -> p k g", k=32),
                    Ebuf.rearrange("p (g k) -> p k g", k=32),
                    rCv.unsqueeze(2).broadcast_to([128, 32, 64]),
                    ALU.mult)))(),
                    reads=(kE, krC), writes=(kC,))
                tloc = t * 128 - local_lo
                P.step("sync", (lambda tloc=tloc, Cbuf=Cbuf: (lambda g: g.dma_start(
                    out=ctm_ap[tloc:tloc + 128, :], in_=Cbuf)))(),
                    reads=(kC,), writes=("C_tm:all",), dma=True)
        for ec in range(16):
            xp = xps[ec % 2]
            kps = f"ps:lgx{ec % 2}"
            pairs = [(Dbuf[:, ti * NE + ec * 128: ti * NE + (ec + 1) * 128],
                      tm.rearrange("p (a f) -> p a f", a=8)[:, ti, :])
                     for ti in range(8)]
            mm_group(P, xp[:], pairs, kps, ("lg:D", ktm))
            if hs == 0:
                P.step("dve", (lambda ec=ec, xp=xp: (lambda v: v.tensor_copy(
                    Xacc[:, ec * DM:(ec + 1) * DM], xp[:])))(),
                    reads=(kps,), writes=(f"lg:Xacc{ec}",))
            else:
                P.step("dve", (lambda ec=ec, xp=xp: (lambda v: v.tensor_add(
                    Xacc[:, ec * DM:(ec + 1) * DM],
                    Xacc[:, ec * DM:(ec + 1) * DM], xp[:])))(),
                    reads=(kps, f"lg:Xacc{ec}"), writes=(f"lg:Xacc{ec}",))

    xt_o = sbuf(P, "lg_xto", [128, 4 * DM], BF16)
    lnb = make_ln_bufs(P, "lgx", 4)
    for eg in range(4):
        for i in range(4):
            ec = eg * 4 + i
            kx = f"lg:Xacc{ec}"
            sl = Xacc[:, ec * DM:(ec + 1) * DM]
            P.step("dve", (lambda sl=sl: (lambda v: v.tensor_scalar(
                sl, sl, -33000.0, 65000.0, ALU.max, ALU.min)))(),
                reads=(kx,), writes=(kx,))
        src = [Xacc[:, (eg * 4 + i) * DM:(eg * 4 + i + 1) * DM] for i in range(4)]
        dst = [xt_o[:, i * DM:(i + 1) * DM] for i in range(4)]
        ln_tiles(P, lnb, src, [f"lg:Xacc{eg * 4 + i}" for i in range(4)],
                 dst, [f"lg:xto{i}" for i in range(4)], act_func=AF.Tanh)
        P.step("sync", (lambda eg=eg: (lambda g: g.dma_start(
            out=xt_ap[eg * 512:(eg + 1) * 512, :].rearrange("(a p) f -> p a f", p=128),
            in_=xt_o[:].rearrange("p (a f) -> p a f", a=4))))(),
            reads=tuple(f"lg:xto{i}" for i in range(4)), writes=("Xt_bf:all",), dma=True)



INPUT_SPECS = {
    "L0_attn_w_qkv": ((DM, 3 * DM), F32),
    "L0_attn_wo": ((DM, DM), F32),
    "L0_attn_lambda_q1": ((HD0,), F32),
    "L0_attn_lambda_q2": ((HD0,), F32),
    "L0_attn_lambda_k1": ((HD0,), F32),
    "L0_attn_lambda_k2": ((HD0,), F32),
    "L0_attn_norm_w": ((H0, HD0), F32),
    "L0_ffn_phi_w": ((NE, DM), F32),
    "L0_ffn_phi_b": ((NE,), F32),
    "L0_ffn_expert_w": ((NEXP, DM, DM), F32),
    "L0_ffn_expert_b": ((NEXP, DM), F32),
    "L1_attn_w_qkv": ((DM, 6 * DM), F32),
    "L1_attn_w_out": ((DM, DM), F32),
    "L1_ffn_w1_w": ((DH, DM), F32),
    "L1_ffn_w1_s": ((DH,), F32),
    "L1_ffn_w2_w": ((DH, DM), F32),
    "L1_ffn_w2_s": ((DH,), F32),
    "L1_ffn_h1_w": ((DH, DH), F32),
    "L1_ffn_h1_s": ((DH,), F32),
    "L1_ffn_h2_w": ((DH, DH), F32),
    "L1_ffn_h2_s": ((DH,), F32),
    "L1_ffn_w3_w": ((DM, DH), F32),
    "L1_ffn_w3_s": ((DM,), F32),
    "x_full": ((TA, DM), F32),
    "x_loc": ((TL, DM), F32),
    "IDT": ((128, 128), BF16),
    "IDTF": ((128, 128), F32),
    "MSL": ((128, 128), F32),
    "MLE": ((128, 128), BF16),
}

INTERM_SPECS = {
    "xn_tm": ((TA, DM), BF16),
    "phi_bf": ((NE, DM), BF16),
    "ew_bf": ((NEXP, DM, DM), BF16),
    "w1_bf": ((DH, DM), BF16),
    "w2_bf": ((DH, DM), BF16),
    "h1_bf": ((DH, DH), BF16),
    "h2_bf": ((DH, DH), BF16),
    "w3_bf": ((DM, DH), BF16),
    "w1p": ((8, 128, DH), BF16),
    "w2p": ((8, 128, DH), BF16),
    "h1p": ((32, 128, DH), BF16),
    "h2p": ((32, 128, DH), BF16),
    "w3p": ((DH, DM), BF16),
    "C_tm": ((TL, NE), BF16),
    "Xt_bf": ((NE, DM), BF16),
    "Yt_bf": ((NE, DM), BF16),
    "qkv0_fm": ((3 * DM, TL), BF16),
    "attn0": ((TL, DM), F32),
    "Cp": ((16, 128, TL), BF16),
    "x1": ((TL, DM), F32),
    "xn1_tm": ((TL, DM), BF16),
    "qkv6_fm": ((6 * DM, TL), BF16),
    "attn1": ((TL, DM), F32),
}

PHASES = {
    "cast": (lambda P, io, st: ph_cast_stream(P, io, st),
             ["L0_ffn_phi_w", "L0_ffn_expert_w", "L1_ffn_w1_w",
              "L1_ffn_w2_w", "L1_ffn_w3_w", "L1_ffn_h1_w", "L1_ffn_h2_w"],
             [], ["phi_bf", "ew_bf", "w1_bf", "w2_bf", "w3_bf",
                  "h1_bf", "h2_bf"]),
    "ln": (lambda P, io, st: ph_lnA(P, io, st, TA, "x_full", "xn_tm", "A",
                                    bgops=ph_panel_ops(P, io, st)),
           ["x_full"], ["w1_bf", "w2_bf", "h1_bf", "h2_bf", "w3_bf"],
           ["xn_tm", "w1p", "w2p", "h1p", "h2p", "w3p"]),
    "logits": (lambda P, io, st: ph_logits(P, io, st, 0, TL),
               ["L0_ffn_phi_b"], ["xn_tm", "phi_bf"], ["C_tm", "Xt_bf"]),
    "qkv0": (lambda P, io, st: ph_qkv0(P, io, st),
             ["L0_attn_w_qkv"], ["xn_tm"], ["qkv0_fm"]),
    "attn0": (lambda P, io, st: ph_attn0(P, io, st),
              ["L0_attn_lambda_q1", "L0_attn_lambda_q2",
               "L0_attn_lambda_k1", "L0_attn_lambda_k2", "L0_attn_norm_w",
               "L0_attn_wo", "IDT", "IDTF"],
              ["qkv0_fm"], ["attn0"]),
    "experts": (lambda P, io, st: ph_experts(P, io, st),
                ["L0_ffn_expert_b"], ["Xt_bf", "ew_bf"], ["Yt_bf"]),
    "ctrans": (lambda P, io, st: ph_ctrans(P, io, st),
               ["IDT"], ["C_tm"], ["Cp"]),
    "y": (lambda P, io, st: ph_y(P, io, st),
          ["x_loc"], ["Cp", "Yt_bf", "attn0"], ["x1", "xn1_tm"]),
    "qkv1": (lambda P, io, st: ph_qkv1(P, io, st),
             ["L1_attn_w_qkv"], ["xn1_tm"], ["qkv6_fm"]),
    "attn1": (lambda P, io, st: ph_attn1(P, io, st),
              ["MSL", "MLE", "L1_attn_w_out"], ["qkv6_fm"], ["attn1"]),
    "ffn1": (lambda P, io, st: ph_ffn1(P, io, st),
             ["L1_ffn_w1_s", "L1_ffn_w2_s", "L1_ffn_h1_s", "L1_ffn_h2_s",
              "L1_ffn_w3_s", "IDTF"],
             ["xn1_tm", "x1", "attn1", "w1p", "w2p", "h1p", "h2p", "w3p"],
             []),
}

PHASE_ORDER = ["cast", "ln", "logits", "qkv0", "attn0",
               "experts", "ctrans", "y", "qkv1", "attn1", "ffn1"]
NO_BARRIER = {"cast"}
OUTER_SCOPE = set()

PHASE_GROUPS = [["cast", "ln"], ["logits"], ["qkv0"], ["attn0"],
                ["experts", "ctrans"], ["y"], ["qkv1"], ["attn1"], ["ffn1"]]


def _add_io(nc, io, name, kind):
    if name in io:
        return
    shape, dt = INPUT_SPECS.get(name) or INTERM_SPECS[name]
    io[name] = nc.dram_tensor(name, list(shape), dt, kind=kind).ap()


def build_group(group):
    nc = bass.Bass("TRN2", target_bir_lowering=False)
    io = {}
    ins, prods, cons = set(), set(), set()
    for ph in group:
        _, in_names, consumed, produced = PHASES[ph]
        ins.update(in_names)
        cons.update(consumed)
        prods.update(produced)
    ext_in = sorted(ins) + sorted(cons - prods)
    last = (group[-1] == "ffn1")
    later_needs = set()
    gi = PHASE_GROUPS.index(group) if group in PHASE_GROUPS else None
    if gi is not None:
        for g2 in PHASE_GROUPS[gi + 1:]:
            for ph in g2:
                later_needs.update(PHASES[ph][2])
    ext_out = sorted(prods & later_needs)
    internal = sorted(prods - later_needs)
    for n in ext_in:
        _add_io(nc, io, n, "ExternalInput")
    for n in ext_out:
        _add_io(nc, io, n, "ExternalOutput")
    for n in internal:
        _add_io(nc, io, n, "Internal")
    if last:
        io["OUT"] = nc.dram_tensor("OUT", [TL, DM], F32, kind="ExternalOutput").ap()
        ext_out = ext_out + ["OUT"]
    io["xnloc_tm"] = io["xn_tm"][0:TL, :] if "xn_tm" in io else None
    with ExitStack() as stack:
        P = Prog(nc, stack)
        for ph in group:
            PHASES[ph][0](P, io, stack)
            P.barrier(full=True)
        block = stack.enter_context(nc.Block())
        P.finalize(block)
    return nc, ext_in, ext_out


def build_fused():
    nc = bass.Bass("TRN2", target_bir_lowering=False)
    io = {}
    for n in INPUT_SPECS:
        _add_io(nc, io, n, "ExternalInput")
    for n in INTERM_SPECS:
        _add_io(nc, io, n, "Internal")
    io["OUT"] = nc.dram_tensor("OUT", [TL, DM], F32, kind="ExternalOutput").ap()
    io["xnloc_tm"] = io["xn_tm"][0:TL, :]
    with ExitStack() as stack:
        P = Prog(nc, stack)
        for ph in PHASE_ORDER:
            if ph in OUTER_SCOPE:
                P.stack = stack
                PHASES[ph][0](P, io, stack)
            else:
                with ExitStack() as phase_stack:
                    P.stack = phase_stack
                    PHASES[ph][0](P, io, phase_stack)
                    if ph not in NO_BARRIER:
                        P.barrier()
        P.stack = stack
        block = stack.enter_context(nc.Block())
        P.finalize(block)
    return nc


def host_consts():
    idt = np.eye(128, dtype=ml_dtypes.bfloat16)
    idtf = np.eye(128, dtype=np.float32)
    c = np.arange(128)
    msl = (c[:, None] > c[None, :]).astype(np.float32)
    mle = (c[:, None] <= c[None, :]).astype(ml_dtypes.bfloat16)
    return {"IDT": idt, "IDTF": idtf, "MSL": msl, "MLE": mle}


FUSED = True


def kernel(**inputs):
    x = np.ascontiguousarray(np.asarray(inputs["L0_attn_x"], dtype=np.float32))
    consts = host_consts()
    base = {k: np.ascontiguousarray(np.asarray(v)) for k, v in inputs.items()
            if k != "L0_attn_x"}
    per_core = []
    for c in range(NCORES):
        order = [2 * c, 2 * c + 1] + [b for b in range(B) if b not in (2 * c, 2 * c + 1)]
        xf = x[order].reshape(TA, DM)
        xl = x[2 * c:2 * c + 2].reshape(TL, DM)
        m = dict(base)
        m.update(consts)
        m["x_full"] = xf
        m["x_loc"] = xl
        per_core.append(m)

    if FUSED:
        nc = build_fused()
        res = run_bass_kernel_spmd(
            nc, [{k: m[k] for k in INPUT_SPECS} for m in per_core],
            core_ids=list(range(NCORES)))
        outs = [r["OUT"] for r in res.results]
    else:
        state = [dict() for _ in range(NCORES)]
        outs = None
        for group in PHASE_GROUPS:
            nc, ext_in, ext_out = build_group(group)
            in_maps = []
            for c in range(NCORES):
                m = {}
                for n in ext_in:
                    m[n] = per_core[c][n] if n in per_core[c] else state[c][n]
                in_maps.append(m)
            res = run_bass_kernel_spmd(nc, in_maps, core_ids=list(range(NCORES)))
            for c in range(NCORES):
                for n in ext_out:
                    state[c][n] = res.results[c][n]
            if "OUT" in ext_out:
                outs = [state[c]["OUT"] for c in range(NCORES)]
    full = np.concatenate([np.asarray(o, dtype=np.float32)[None] for o in outs], axis=0)
    return full.reshape(B, S, DM)
